# version 03
# baseline (speedup 1.0000x reference)
import numpy as np
import ml_dtypes
import concourse.bass as bass
import concourse.mybir as mybir
from concourse.bass_utils import run_bass_kernel_spmd

BF = mybir.dt.bfloat16
F32 = mybir.dt.float32
I32 = mybir.dt.int32
U8 = mybir.dt.uint8
AF = mybir.ActivationFunctionType
ALU = mybir.AluOpType
AX = mybir.AxisListType

T_TOK = 2048
D = 2048
NT = 16
NP = 8
NCH = 16
NHT = 64
TCH = [768, 768, 512]
TOFF = [0, 768, 1536]
PSL = 768

DEBUG = False
TAGS = []


def build_nc():
    nc = bass.Bass("TRN2", target_bir_lowering=False,
                   detect_race_conditions=False)

    x_d = nc.dram_tensor("x", [T_TOK, D], F32, kind="ExternalInput")
    aw_d = nc.dram_tensor("attn_w", [56, 16, 128, 128], F32, kind="ExternalInput")
    wo_d = nc.dram_tensor("wo_w", [16, 128, D], F32, kind="ExternalInput")
    wg_d = nc.dram_tensor("wg_s", [64, 16, 128, 128], F32, kind="ExternalInput")
    wu_d = nc.dram_tensor("wu_s", [64, 16, 128, 128], F32, kind="ExternalInput")
    wd_d = nc.dram_tensor("wd_s", [16, 64, 128, 128], F32, kind="ExternalInput")
    gnw_d = nc.dram_tensor("gnw", [128, 1], F32, kind="ExternalInput")
    ib_d = nc.dram_tensor("ident_bf", [128, 128], BF, kind="ExternalInput")
    if_d = nc.dram_tensor("ident_f32", [128, 128], F32, kind="ExternalInput")
    tri_d = nc.dram_tensor("triu_bf", [128, 128], BF, kind="ExternalInput")
    out_d = nc.dram_tensor("out", [T_TOK, D], F32, kind="ExternalOutput")
    xq_sp = nc.dram_tensor("xq_sp", [NT, 128, T_TOK], BF, kind="Internal")
    wgb = nc.dram_tensor("wgb", [64, 16, 128, 128], BF, kind="Internal")
    wub = nc.dram_tensor("wub", [64, 16, 128, 128], BF, kind="Internal")
    wdb = nc.dram_tensor("wdb", [16, 64, 128, 128], BF, kind="Internal")
    sat_sp = nc.dram_tensor("sat_sp", [NT, 128, D], BF, kind="Internal")

    dbg = {}
    if DEBUG:
        dbg["xnT"] = nc.dram_tensor("dbg_xnT", [NT, 128, T_TOK], BF, kind="ExternalOutput")
        dbg["xq"] = nc.dram_tensor("dbg_xq", [NT, 128, T_TOK], BF, kind="ExternalOutput")
        dbg["oall"] = nc.dram_tensor("dbg_oall", [16, 128, T_TOK], BF, kind="ExternalOutput")
        dbg["sat"] = nc.dram_tensor("dbg_sat", [NT, 128, D], BF, kind="ExternalOutput")
        dbg["st"] = nc.dram_tensor("dbg_st", [128, 6 * NT], F32, kind="ExternalOutput")

    from contextlib import ExitStack
    es = ExitStack()

    def sbuf(name, shape, dt):
        return es.enter_context(nc.sbuf_tensor(name, shape, dt))

    xnA = sbuf("xnA", [128, NT * T_TOK], BF)
    oaA = sbuf("oaA", [128, NT * T_TOK], BF)
    GF_OFF = 0
    XQC_OFF = 12288
    GT_OFF = 24576
    SAT_OFF = 26624
    POV_OFF = 28672

    cIb = sbuf("cIb", [128, 128], BF)
    cIf = sbuf("cIf", [128, 128], F32)
    cTri = sbuf("cTri", [128, 128], BF)
    cGnw = sbuf("cGnw", [128, 1], F32)
    cZ = sbuf("cZ", [128, 1], F32)

    stats = {}
    for nm in ("s1", "s2", "m", "r", "nb", "s2q", "amax", "r2", "sq", "inv",
               "sq2r", "t1"):
        stats[nm] = sbuf("st_" + nm, [128, NT], F32)
    smx = sbuf("smx", [128, 2 * NCH], F32)
    ssm = sbuf("ssm", [128, 2 * NCH], F32)
    srs = sbuf("srs", [128, 2 * NCH], F32)
    sos = sbuf("sos", [128, 2 * NCH], F32)
    sor = sbuf("sor", [128, 2 * NCH], F32)

    f32a = sbuf("f32a", [128, 2048], F32)
    f32b = sbuf("f32b", [128, 2048], F32)
    okn = sbuf("okn", [128, 2048], F32)

    AR = 54272
    arena = sbuf("arena", [128, AR], U8)
    ar_base = nc.lookup_mloc(arena).addr

    def at(name, off, shape, dt):
        return nc.alloc_sbuf_tensor_at(name, shape, dt, offset=ar_base + off)

    qqT = at("qqT", 0, [128, 2048], BF)
    kkpw = at("kkpw", 4096, [128, 2048], BF)
    embf = at("embf", 8192, [128, 2048], BF)
    etok = at("etok", 12288, [128, 2048], BF)
    ktok = at("ktok", 16384, [128, 2048], BF)
    vt0 = at("vt0", 20480, [128, 2048], BF)
    vt1 = at("vt1", 24576, [128, 2048], BF)
    gs0 = at("gs0", 28672, [128, 2048], BF)
    gs1 = at("gs1", 32768, [128, 2048], BF)
    sm0 = at("sm0", 36864, [128, 2048], BF)
    sm1 = at("sm1", 40960, [128, 2048], BF)
    pwT = at("pwT", 45056, [128, 2048], BF)
    scr4 = at("scr4", 49152, [128, 1024], F32)
    xqi = at("xqi", 20480, [128, 2048], I32)
    pext = at("pext", 0, [128, 17 * PSL], BF)
    wgl = [at("wgl0", 26624, [128, 2048], BF), at("wgl1", 30720, [128, 2048], BF)]
    wul = [at("wul0", 34816, [128, 2048], BF), at("wul1", 38912, [128, 2048], BF)]
    wdl = [at("wdl0", 43008, [128, 2048], BF), at("wdl1", 47104, [128, 2048], BF)]
    gsil = at("gsil", 51200, [128, 768], BF)
    gsilB = at("gsilB", 52736, [128, 768], BF)

    ps4a = es.enter_context(nc.psum_tensor("ps4a", [128, 2048], F32))
    ps2a = es.enter_context(nc.psum_tensor("ps2a", [128, 1024], F32))
    ps2b = es.enter_context(nc.psum_tensor("ps2b", [128, 1024], F32))

    sem_names = ["T", "V", "S", "YC",
                 "XA", "XB", "XQ0", "XQ1",
                 "WA0", "WA1", "GWO", "SP0", "SP1",
                 "Q4", "WG0", "WG1", "WU0", "WU1", "FD0", "FD1", "PP",
                 "CX0", "CS0", "CO0"]
    sems = {n: nc.alloc_semaphore(name="sem_" + n) for n in sem_names}

    def zbc(n):
        return bass.AP(cZ, 0, [[1, 128], [0, n]])

    def tri_bc(nch):
        return bass.AP(cTri, 0, [[128, 128], [0, nch], [1, 128]])

    def r3(ap, b):
        return ap.rearrange("p (a b) -> p a b", b=b)

    def p_slab(ht):
        if ht < 42:
            return bass.AP(xnA, ht * PSL, [[NT * T_TOK, 128], [1, PSL]])
        if ht < 47:
            return bass.AP(oaA, POV_OFF + (ht - 42) * PSL, [[NT * T_TOK, 128], [1, PSL]])
        return bass.AP(pext, (ht - 47) * PSL, [[17 * PSL, 128], [1, PSL]])

    def oa(off, n):
        return bass.AP(oaA, off, [[NT * T_TOK, 128], [1, n]])

    def prog(me, eng):
        C = {k: 0 for k in sems}
        LW = {k: 0 for k in sems}
        _tmm = [0]
        rec = {}

        def tag(name):
            if me == "t":
                TAGS.append((name, _tmm[0]))

        def done(res, engsem):
            rec[res] = (engsem, C[engsem])

        def need(who, *ress):
            for res in ress:
                e_, c_ = rec.get(res, (None, 0))
                if e_ is not None:
                    WT(who, e_, c_)

        def W(who, *names):
            if who != me:
                return
            for n_ in names:
                if C[n_] > 0 and LW[n_] < C[n_]:
                    eng.wait_ge(sems[n_], C[n_])
                    LW[n_] = C[n_]

        def WT(who, sem, val):
            if who == me and val > 0 and LW[sem] < val:
                eng.wait_ge(sems[sem], val)
                LW[sem] = val

        def S(owner, sem, n, f):
            inst = f(eng) if owner == me else None
            if owner == "t":
                _tmm[0] += 1
            if sem is not None:
                C[sem] += n
                if inst is not None:
                    inst.then_inc(sems[sem], n)
            return C[sem] if sem is not None else None

        S("y", "YC", 16, lambda e: e.dma_start(cIb[:], ib_d[:, :]))
        S("y", "YC", 16, lambda e: e.dma_start(cIf[:], if_d[:, :]))
        S("y", "YC", 16, lambda e: e.dma_start(cTri[:], tri_d[:, :]))
        S("y", "YC", 16, lambda e: e.dma_start(cGnw[:], gnw_d[:, :]))
        W("v", "YC")
        S("v", "V", 1, lambda e: e.memset(cZ[:], 0.0))

        for k in range(8):
            S("g", "PP", 16, lambda e, k=k: e.dma_start(
                wgb[k * 8:(k + 1) * 8, :, :, :], wg_d[k * 8:(k + 1) * 8, :, :, :]))
            S("g", "PP", 16, lambda e, k=k: e.dma_start(
                wub[k * 8:(k + 1) * 8, :, :, :], wu_d[k * 8:(k + 1) * 8, :, :, :]))
            S("g", "PP", 16, lambda e, k=k: e.dma_start(
                wdb[k * 2:(k + 1) * 2, :, :, :], wd_d[k * 2:(k + 1) * 2, :, :, :]))

        tag("P1")
        xslot = [f32a, f32b]
        xsem = ["XA", "XB"]
        xfree = [0, 0]
        qstg = [etok, ktok]
        qsem = ["XQ0", "XQ1"]
        qfree = [0, 0]

        for i in range(NT):
            sl = i % 2
            xt = xslot[sl]
            WT("y", "V", xfree[sl])
            S("y", xsem[sl], 16, lambda e, i=i, xt=xt: e.dma_start(
                xt[:], x_d[i * 128:(i + 1) * 128, :]))

            W("s", xsem[sl])
            S("s", "S", 1, lambda e, i=i, xt=xt: e.activation(
                oa(15 * T_TOK, 2048), xt[:], AF.Square,
                accum_out=stats["s2"][:, i:i + 1]))
            W("v", xsem[sl])
            S("v", "V", 1, lambda e, i=i, xt=xt: e.reduce_sum(
                stats["s1"][:, i:i + 1], xt[:], axis=AX.X))
            W("v", "V")
            S("v", "V", 1, lambda e, i=i: e.tensor_scalar_mul(
                stats["m"][:, i:i + 1], stats["s1"][:, i:i + 1], 1.0 / D))
            W("v", "V")
            S("v", "V", 1, lambda e, i=i: e.tensor_mul(
                stats["t1"][:, i:i + 1], stats["m"][:, i:i + 1], stats["m"][:, i:i + 1]))
            W("v", "S")
            S("v", "V", 1, lambda e, i=i: e.tensor_scalar_mul(
                stats["r"][:, i:i + 1], stats["s2"][:, i:i + 1], 1.0 / D))
            W("v", "V")
            S("v", "V", 1, lambda e, i=i: e.tensor_sub(
                stats["r"][:, i:i + 1], stats["r"][:, i:i + 1], stats["t1"][:, i:i + 1]))
            W("v", "V")
            S("v", "V", 1, lambda e, i=i: e.tensor_scalar_add(
                stats["r"][:, i:i + 1], stats["r"][:, i:i + 1], 1e-5))
            W("s", "V")
            S("s", "S", 1, lambda e, i=i: e.activation(
                stats["r"][:, i:i + 1], stats["r"][:, i:i + 1], AF.Sqrt))
            W("v", "S")
            S("v", "V", 1, lambda e, i=i: e.reciprocal(
                stats["r"][:, i:i + 1], stats["r"][:, i:i + 1]))
            W("v", "V")
            S("v", "V", 1, lambda e, i=i: e.tensor_mul(
                stats["nb"][:, i:i + 1], stats["m"][:, i:i + 1], stats["r"][:, i:i + 1]))
            W("v", "V")
            S("v", "V", 1, lambda e, i=i: e.tensor_scalar_mul(
                stats["nb"][:, i:i + 1], stats["nb"][:, i:i + 1], -1.0))
            W("v", "V")
            S("v", "V", 1, lambda e, i=i, xt=xt: e.scalar_tensor_tensor(
                okn[:], xt[:], stats["r"][:, i:i + 1],
                bass.AP(stats["nb"], i, [[NT, 128], [0, 2048]]),
                ALU.mult, ALU.add))
            W("s", "V")
            S("s", "S", 1, lambda e, i=i: e.activation(
                oa(15 * T_TOK, 2048), okn[:], AF.Square,
                accum_out=stats["s2q"][:, i:i + 1]))
            W("v", "S")
            S("v", "V", 1, lambda e, i=i: e.reduce_max(
                stats["amax"][:, i:i + 1], okn[:], axis=AX.X,
                apply_absolute_value=True))
            S("v", "V", 1, lambda e: e.tensor_copy(sm0[:], okn[:]))
            S("v", "V", 1, lambda e, i=i: e.tensor_scalar(
                stats["r2"][:, i:i + 1], stats["s2q"][:, i:i + 1],
                1.0 / D, 1e-8, ALU.mult, ALU.add))
            W("s", "V")
            S("s", "S", 1, lambda e, i=i: e.activation(
                stats["r2"][:, i:i + 1], stats["r2"][:, i:i + 1], AF.Sqrt))
            W("v", "S")
            S("v", "V", 1, lambda e, i=i: e.reciprocal(
                stats["r2"][:, i:i + 1], stats["r2"][:, i:i + 1]))
            W("v", "V")
            S("v", "V", 1, lambda e, i=i: e.tensor_mul(
                stats["amax"][:, i:i + 1], stats["amax"][:, i:i + 1],
                stats["r2"][:, i:i + 1]))
            W("v", "V")
            S("v", "V", 1, lambda e, i=i: e.reciprocal(
                stats["sq"][:, i:i + 1], stats["amax"][:, i:i + 1]))
            W("v", "V")
            S("v", "V", 1, lambda e, i=i: e.tensor_scalar_mul(
                stats["sq"][:, i:i + 1], stats["sq"][:, i:i + 1], 127.0))
            W("v", "V")
            S("v", "V", 1, lambda e, i=i: e.reciprocal(
                stats["inv"][:, i:i + 1], stats["sq"][:, i:i + 1]))
            W("v", "V")
            S("v", "V", 1, lambda e, i=i: e.tensor_mul(
                stats["sq2r"][:, i:i + 1], stats["sq"][:, i:i + 1],
                stats["r2"][:, i:i + 1]))
            W("v", "V")
            S("v", "V", 1, lambda e, i=i, xt=xt: e.tensor_scalar_mul(
                xt[:], okn[:], stats["sq2r"][:, i:i + 1]))
            S("v", "V", 1, lambda e, xt=xt: e.tensor_copy(xqi[:], xt[:]))
            vq = S("v", "V", 1, lambda e, i=i: e.tensor_scalar_mul(
                sm1[:], xqi[:], stats["inv"][:, i:i + 1]))
            xfree[sl] = vq

            W("t", "YC")
            for grp in range(4):
                pp = ps2a if grp % 2 == 0 else ps2b
                W("t", "V")
                for j in range(4):
                    dt = grp * 4 + j
                    S("t", "T" if j == 3 else None, 1, lambda e, dt=dt, j=j, pp=pp: e.matmul(
                        pp[:, j * 128:(j + 1) * 128],
                        sm0[:, dt * 128:(dt + 1) * 128], cIb[:],
                        start=True, stop=True))
                W("v", "T")
                S("v", "V", 1, lambda e, grp=grp, pp=pp, i=i: e.tensor_copy(
                    bass.AP(xnA, (grp * 4) * T_TOK + i * 128,
                            [[NT * T_TOK, 128], [T_TOK, 4], [1, 128]]),
                    r3(pp[:, 0:512], 128)))
            done("xnT%d" % i, "V")
            qsl = i % 2
            stg = qstg[qsl]
            for grp in range(4):
                pp = ps2a if grp % 2 == 0 else ps2b
                W("t", "V")
                WT("t", qsem[qsl], qfree[qsl])
                for j in range(4):
                    dt = grp * 4 + j
                    S("t", "T" if j == 3 else None, 1, lambda e, dt=dt, j=j, pp=pp: e.matmul(
                        pp[:, j * 128:(j + 1) * 128],
                        sm1[:, dt * 128:(dt + 1) * 128], cIb[:],
                        start=True, stop=True))
                W("v", "T")
                S("v", "V", 1, lambda e, grp=grp, pp=pp, stg=stg: e.tensor_copy(
                    stg[:, grp * 512:(grp + 1) * 512], pp[:, 0:512]))
            W("y", "V")
            qfree[qsl] = S("y", qsem[qsl], 16, lambda e, i=i, stg=stg: e.dma_start(
                xq_sp[:, :, i * 128:(i + 1) * 128].rearrange("a p b -> p a b"),
                r3(stg[:], 128)))

        if DEBUG:
            W("y", "V")
            S("y", "YC", 16, lambda e: e.dma_start(
                dbg["xnT"][:, :, :].rearrange("a p b -> p a b"),
                r3(xnA[:], T_TOK)))
            for k_, nm_ in enumerate(("s1", "s2", "m", "r", "nb", "sq2r")):
                S("y", "YC", 16, lambda e, k_=k_, nm_=nm_: e.dma_start(
                    dbg["st"][:, k_ * NT:(k_ + 1) * NT], stats[nm_][:, :]))
            W("y", "XQ0", "XQ1")
            S("y", "YC", 16, lambda e: e.dma_start(
                dbg["xq"][:, :, :], xq_sp[:, :, :]))

        tag("P2")
        wfree = [0, 0]

        def wp_ap(p, par, tile=None):
            off = (2 * p + par) * T_TOK
            if tile is None:
                return bass.AP(oaA, off, [[NT * T_TOK, 128], [1, 2048]])
            return bass.AP(oaA, off + tile * 128, [[NT * T_TOK, 128], [1, 128]])

        def tgroup(src, dst, fp32, src_res=None, dst_res=None, who_evac="s"):
            esem = "V" if who_evac == "v" else "S"
            for grp in range(4):
                pp = ps2a if grp % 2 == 0 else ps2b
                ppres = "ps2a" if grp % 2 == 0 else "ps2b"
                if src_res:
                    need("t", src_res)
                need("t", ppres)
                for j in range(4):
                    c = grp * 4 + j
                    if fp32:
                        S("t", "T" if j == 3 else None, 1,
                          lambda e, c=c, j=j, pp=pp, src=src: e.transpose(
                              pp[:, j * 128:(j + 1) * 128],
                              src[:, c * 128:(c + 1) * 128], cIf[:]))
                    else:
                        S("t", "T" if j == 3 else None, 1,
                          lambda e, c=c, j=j, pp=pp, src=src: e.matmul(
                              pp[:, j * 128:(j + 1) * 128],
                              src[:, c * 128:(c + 1) * 128], cIb[:],
                              start=True, stop=True))
                W(who_evac, "T")
                if who_evac == "s":
                    S("s", "S", 1, lambda e, grp=grp, pp=pp, dst=dst: e.copy(
                        dst[:, grp * 512:(grp + 1) * 512], pp[:, 0:512]))
                else:
                    S("v", "V", 1, lambda e, grp=grp, pp=pp, dst=dst: e.tensor_copy(
                        dst[:, grp * 512:(grp + 1) * 512], pp[:, 0:512]))
                done(ppres, esem)
            if dst_res:
                done(dst_res, esem)

        def wp_load(p, f):
            par = f % 2
            WT("g", "T", wfree[par])
            S("g", ["WA0", "WA1"][par], 16, lambda e, p=p, f=f, par=par: e.dma_start(
                r3(wp_ap(p, par), 128),
                aw_d[p * 7 + f, :, :, :].rearrange("a p b -> p a b")))

        for p in range(NP):
            tag(f"P2.proj")
            wfree[0] = wfree[1] = 0
            wp_load(p, 0)
            wp_load(p, 1)
            for f in range(7):
                par = f % 2
                W("t", ["WA0", "WA1"][par])
                for n in range(4):
                    if p == 0:
                        need("t", "xnT%d" % (4 * n + 3))
                    need("t", "psA%d" % n)
                    for dt in range(NT):
                        S("t", "T" if dt == NT - 1 else None, 1,
                          lambda e, n=n, dt=dt, p=p, par=par: e.matmul(
                              ps4a[:, n * 512:(n + 1) * 512],
                              wp_ap(p, par, dt),
                              bass.AP(xnA, dt * T_TOK + n * 512, [[NT * T_TOK, 128], [1, 512]]),
                              start=(dt == 0), stop=(dt == NT - 1)))
                    if n == 3:
                        wfree[par] = C["T"]
                        if f + 2 < 7:
                            wp_load(p, f + 2)
                    pc = ps4a[:, n * 512:(n + 1) * 512]
                    if f == 2:
                        W("v", "T")
                        S("v", "V", 1, lambda e, n=n, pc=pc: e.tensor_scalar(
                            f32a[:, n * 512:(n + 1) * 512], pc,
                            -32.0, 32.0, ALU.max, ALU.min))
                        done("psA%d" % n, "V")
                    else:
                        W("s", "T")
                        if f == 0:
                            S("s", "S", 1, lambda e, n=n, pc=pc: e.mul(
                                qqT[:, n * 512:(n + 1) * 512], pc, 0.125))
                        elif f == 1:
                            S("s", "S", 1, lambda e, n=n, pc=pc: e.copy(
                                kkpw[:, n * 512:(n + 1) * 512], pc))
                        elif f in (3, 4):
                            vtb = vt0 if f == 3 else vt1
                            S("s", "S", 1, lambda e, n=n, pc=pc, vtb=vtb: e.copy(
                                vtb[:, n * 512:(n + 1) * 512], pc))
                        else:
                            gsb = gs0 if f == 5 else gs1
                            S("s", "S", 1, lambda e, n=n, pc=pc, gsb=gsb: e.activation(
                                gsb[:, n * 512:(n + 1) * 512], pc, AF.Silu))
                        done("psA%d" % n, "S")
                if f == 0:
                    done("qqT", "S")
                elif f == 1:
                    done("kkT", "S")
                elif f == 2:
                    done("sclip", "V")
                elif f == 3:
                    done("vt0", "S")
                elif f == 4:
                    done("vt1", "S")
                elif f == 5:
                    done("gs0", "S")
                elif f == 6:
                    done("gs1", "S")
                if f == 2:
                    need("s", "sclip")
                    S("s", "S", 1, lambda e: e.activation(f32b[:], f32a[:], AF.Exp))
                    done("E32", "S")
                    need("v", "E32")
                    S("v", "V", 1, lambda e: e.tensor_tensor_scan(
                        f32a[:], f32b[:], zbc(2048), 0.0, ALU.add, ALU.add))
                    S("v", "V", 1, lambda e: e.tensor_copy(embf[:], f32b[:]))
                    done("embf", "V")
                    S("v", "V", 1, lambda e: e.reciprocal(f32b[:], f32a[:]))
                    done("recipAm", "V")

            tag("P2.tr")
            if p == 0:
                W("s", "XQ0", "XQ1")
            tgroup(embf, etok, False, "embf", "etok")
            tgroup(kkpw, ktok, False, "kkT", "Ktok")
            for vtb, stg, rr in ((vt0, sm0, "vt0"), (vt1, sm1, "vt1")):
                tgroup(vtb, stg, False, rr, rr + "st")
                need("v", rr + "st")
                S("v", "V", 1, lambda e, vtb=vtb, stg=stg: e.tensor_copy(vtb[:], stg[:]))
                done(rr + "tok", "V")
            tgroup(f32b, f32a, True, "recipAm", "recipA")

            tag("P2.pk")
            for rnd in range(2):
                need("t", "Ktok", "etok", "ps2b")
                if rnd == 1:
                    W("t", "V")
                for cc in range(8):
                    c = rnd * 8 + cc
                    S("t", "T" if cc == 7 else None, 1, lambda e, c=c, cc=cc: e.matmul(
                        ps2b[:, cc * 128:(cc + 1) * 128],
                        ktok[:, c * 128:(c + 1) * 128],
                        etok[:, c * 128:(c + 1) * 128],
                        start=True, stop=True))
                W("v", "T")
                if rnd == 0:
                    S("v", "V", 1, lambda e: e.memset(f32b[:, 0:128], 0.0))
                for cc in range(8):
                    c = rnd * 8 + cc
                    if c == NCH - 1:
                        break
                    S("v", "V", 1, lambda e, c=c, cc=cc: e.tensor_add(
                        f32b[:, (c + 1) * 128:(c + 2) * 128],
                        f32b[:, c * 128:(c + 1) * 128],
                        ps2b[:, cc * 128:(cc + 1) * 128]))
            S("v", "V", 1, lambda e: e.tensor_copy(ktok[:], f32b[:]))
            done("Hk", "V")
            done("ps2b", "V")

            tag("P2.sok")
            for h in range(2):
                smh = sm0 if h == 0 else sm1
                pso = ps2a if h == 0 else ps2b
                okh = okn[:, h * 1024:(h + 1) * 1024]
                for c in range(NCH):
                    if c % 4 == 0:
                        need("t", "kkT", "qqT", "psA%d" % (c // 4))
                    S("t", "T" if c % 4 == 3 else None, 1, lambda e, c=c, h=h: e.matmul(
                        ps4a[:, c * 128:(c + 1) * 128],
                        kkpw[h * 64:(h + 1) * 64, c * 128:(c + 1) * 128],
                        qqT[h * 64:(h + 1) * 64, c * 128:(c + 1) * 128],
                        start=True, stop=True))
                    if c % 4 == 3:
                        W("v", "T")
                        g4 = c // 4
                        S("v", "V", 1, lambda e, g4=g4, smh=smh: e.tensor_tensor(
                            r3(smh[:, g4 * 512:(g4 + 1) * 512], 128),
                            r3(ps4a[:, g4 * 512:(g4 + 1) * 512], 128),
                            tri_bc(4), ALU.mult))
                        done("psA%d" % g4, "V")
                        done("smm%d_%d" % (h, g4), "V")
                need("t", "etok", "Hk", "ps2a" if h == 0 else "ps2b")
                for c in range(NCH):
                    last = (c == NCH - 1)
                    if c % 4 == 0:
                        need("t", "smm%d_%d" % (h, c // 4))
                    S("t", None, 1, lambda e, c=c, h=h, smh=smh, pso=pso: e.matmul(
                        pso[:, c * 64:(c + 1) * 64],
                        smh[:, c * 128:(c + 1) * 128],
                        etok[:, c * 128 + h * 64: c * 128 + h * 64 + 64],
                        start=True, stop=(c == 0)))
                    if c > 0:
                        S("t", "T" if last else None, 1, lambda e, c=c, h=h: e.matmul(
                            pso[:, c * 64:(c + 1) * 64],
                            qqT[h * 64:(h + 1) * 64, c * 128:(c + 1) * 128],
                            ktok[h * 64:(h + 1) * 64, c * 128 + h * 64: c * 128 + h * 64 + 64],
                            start=False, stop=True))
                W("v", "T")
                need("v", "recipA")
                S("v", "V", 1, lambda e, okh=okh, h=h, pso=pso: e.tensor_tensor(
                    r3(okh, 64), r3(pso[:], 64),
                    bass.AP(f32a, h * 64, [[2048, 128], [128, NCH], [1, 64]]),
                    ALU.mult))
                done("ps2a" if h == 0 else "ps2b", "V")
                S("v", "V", 1, lambda e, okh=okh, h=h: e.reduce_max(
                    smx[:, h * NCH:(h + 1) * NCH], r3(okh, 64),
                    axis=AX.X, negate=True))
                done("smx%d" % h, "V")
            for h in range(2):
                okh = okn[:, h * 1024:(h + 1) * 1024]
                need("s", "smx%d" % h)
                if h == 1:
                    W("s", "V")
                for c in range(NCH):
                    S("s", "S" if c == NCH - 1 else None, 1, lambda e, c=c, okh=okh, h=h: e.activation(
                        scr4[:, c * 64:(c + 1) * 64], okh[:, c * 64:(c + 1) * 64],
                        AF.Exp, bias=smx[:, h * NCH + c:h * NCH + c + 1],
                        accum_out=ssm[:, h * NCH + c:h * NCH + c + 1]))
                done("exp%d" % h, "S")
                need("v", "exp%d" % h)
                W("v", "T")
                W("v", "V")
                S("v", "V", 1, lambda e, h=h: e.reciprocal(
                    srs[:, h * NCH:(h + 1) * NCH], ssm[:, h * NCH:(h + 1) * NCH]))
                W("v", "V")
                S("v", "V", 1, lambda e, okh=okh, h=h: e.tensor_tensor(
                    r3(okh, 64),
                    bass.AP(f32a, h * 64, [[2048, 128], [128, NCH], [1, 64]]),
                    bass.AP(srs, h * NCH, [[2 * NCH, 128], [1, NCH], [0, 64]]),
                    ALU.mult))
                S("v", "V", 1, lambda e, okh=okh, h=h: e.tensor_tensor(
                    bass.AP(kkpw, h * 64, [[2048, 128], [128, NCH], [1, 64]]),
                    r3(scr4[:], 64), r3(okh, 64), ALU.mult))

            done("pw", "V")
            tag("P2.pw")
            tgroup(kkpw, pwT, False, "pw", "pwT")

            for rnd in range(2):
                need("t", "etok", "vt0tok", "vt1tok", "ps2b")
                W("t", "V")
                for cc in range(8):
                    c = rnd * 8 + cc
                    S("t", None, 1, lambda e, c=c, cc=cc: e.matmul(
                        ps2b[0:64, cc * 128:(cc + 1) * 128],
                        etok[:, c * 128: c * 128 + 64],
                        vt0[:, c * 128:(c + 1) * 128],
                        start=True, stop=True))
                    S("t", "T" if cc == 7 else None, 1, lambda e, c=c, cc=cc: e.matmul(
                        ps2b[64:128, cc * 128:(cc + 1) * 128],
                        etok[:, c * 128 + 64: c * 128 + 128],
                        vt1[:, c * 128:(c + 1) * 128],
                        start=True, stop=True))
                W("v", "T")
                if rnd == 0:
                    S("v", "V", 1, lambda e: e.memset(f32b[:, 0:128], 0.0))
                for cc in range(8):
                    c = rnd * 8 + cc
                    if c == NCH - 1:
                        break
                    S("v", "V", 1, lambda e, c=c, cc=cc: e.tensor_add(
                        f32b[:, (c + 1) * 128:(c + 2) * 128],
                        f32b[:, c * 128:(c + 1) * 128],
                        ps2b[:, cc * 128:(cc + 1) * 128]))
            S("v", "V", 1, lambda e: e.tensor_copy(qqT[:], f32b[:]))
            done("Hv", "V")
            done("ps2b", "V")

            tag("P2.ov")
            for h in range(2):
                smh = sm0 if h == 0 else sm1
                vth = vt0 if h == 0 else vt1
                gsh = gs0 if h == 0 else gs1
                hh = 2 * p + h
                for c in range(NCH):
                    if c % 4 == 0:
                        need("t", "embf", "pwT", "psA%d" % (c // 4))
                    S("t", "T" if c % 4 == 3 else None, 1, lambda e, c=c, h=h: e.matmul(
                        ps4a[:, c * 128:(c + 1) * 128],
                        embf[h * 64:(h + 1) * 64, c * 128:(c + 1) * 128],
                        pwT[h * 64:(h + 1) * 64, c * 128:(c + 1) * 128],
                        start=True, stop=True))
                    if c % 4 == 3:
                        W("v", "T")
                        g4 = c // 4
                        S("v", "V", 1, lambda e, g4=g4, smh=smh: e.tensor_tensor(
                            r3(smh[:, g4 * 512:(g4 + 1) * 512], 128),
                            r3(ps4a[:, g4 * 512:(g4 + 1) * 512], 128),
                            tri_bc(4), ALU.mult))
                        done("psA%d" % g4, "V")
                        done("smm%d_%d" % (h, g4), "V")
                need("t", "Hv", "vt%dtok" % h)
                for c in range(NCH):
                    last = (c == NCH - 1)
                    if c % 4 == 0:
                        need("t", "smm%d_%d" % (h, c // 4), "psA%d" % (c // 4))
                    S("t", None, 1, lambda e, c=c, smh=smh, vth=vth: e.matmul(
                        ps4a[:, c * 128:(c + 1) * 128],
                        smh[:, c * 128:(c + 1) * 128],
                        vth[:, c * 128:(c + 1) * 128],
                        start=True, stop=(c == 0)))
                    if c > 0:
                        S("t", "T" if last else None, 1, lambda e, c=c, h=h: e.matmul(
                            ps4a[:, c * 128:(c + 1) * 128],
                            pwT[h * 64:(h + 1) * 64, c * 128:(c + 1) * 128],
                            qqT[h * 64:(h + 1) * 64, c * 128:(c + 1) * 128],
                            start=False, stop=True))
                W("s", "T")
                need("s", "Hv")
                S("s", "S", 1, lambda e: e.activation(f32b[:], ps4a[:], AF.Square))
                W("v", "S")
                S("v", "V", 1, lambda e, h=h: e.reduce_sum(
                    sos[:, h * NCH:(h + 1) * NCH], r3(f32b[:], 128), axis=AX.X))
                W("v", "V")
                S("v", "V", 1, lambda e, h=h: e.tensor_scalar(
                    sor[:, h * NCH:(h + 1) * NCH], sos[:, h * NCH:(h + 1) * NCH],
                    1.0 / 128.0, 1e-5, ALU.mult, ALU.add))
                W("s", "V")
                S("s", "S", 1, lambda e, h=h: e.activation(
                    sor[:, h * NCH:(h + 1) * NCH], sor[:, h * NCH:(h + 1) * NCH],
                    AF.Sqrt))
                W("v", "S")
                W("v", "V")
                S("v", "V", 1, lambda e, h=h: e.reciprocal(
                    sor[:, h * NCH:(h + 1) * NCH], sor[:, h * NCH:(h + 1) * NCH]))
                W("v", "V")
                S("v", "V", 1, lambda e, h=h, smh=smh: e.tensor_tensor(
                    r3(smh[:], 128), r3(ps4a[:], 128),
                    bass.AP(sor, h * NCH, [[2 * NCH, 128], [1, NCH], [0, 128]]),
                    ALU.mult))
                for grp in range(4):
                    pp = ps2a if grp % 2 == 0 else ps2b
                    W("t", "V", "S")
                    for j in range(4):
                        c = grp * 4 + j
                        S("t", "T" if j == 3 else None, 1,
                          lambda e, c=c, j=j, pp=pp, smh=smh: e.matmul(
                              pp[:, j * 128:(j + 1) * 128],
                              smh[:, c * 128:(c + 1) * 128], cIb[:],
                              start=True, stop=True))
                    W("v", "T")
                    S("v", "V", 1, lambda e, grp=grp, pp=pp, gsh=gsh, hh=hh: e.scalar_tensor_tensor(
                        oa(hh * T_TOK + grp * 512, 512),
                        pp[:, 0:512], cGnw[:, 0:1],
                        gsh[:, grp * 512:(grp + 1) * 512],
                        ALU.mult, ALU.mult))

        p2_T = C["T"]

        if DEBUG:
            W("y", "V")
            S("y", "YC", 16, lambda e: e.dma_start(
                dbg["oall"][:, :, :].rearrange("a p b -> p a b"),
                r3(oaA[:], T_TOK)))

        tag("P3")
        WT("g", "T", p2_T)
        for vt in range(16):
            S("g", "GWO", 16, lambda e, vt=vt: e.dma_start(
                bass.AP(xnA, vt * T_TOK, [[NT * T_TOK, 128], [1, D]]),
                wo_d[vt, :, :]))
        sslot = [qqT, kkpw]
        ssemn = ["SP0", "SP1"]
        sfree = [0, 0]
        for tt in range(NT):
            sl = tt % 2
            stg = sslot[sl]
            W("t", "GWO")
            W("t", "V")
            for vt in range(16):
                for n in range(4):
                    if sl == 0:
                        tgt = ps4a[:, n * 512:(n + 1) * 512]
                    else:
                        tgt = (ps2a if n < 2 else ps2b)[:, (n % 2) * 512:((n % 2) + 1) * 512]
                    S("t", "T" if (vt == 15 and n == 3) else None, 1,
                      lambda e, vt=vt, n=n, tt=tt, tgt=tgt: e.matmul(
                          tgt,
                          oa(vt * T_TOK + tt * 128, 128),
                          bass.AP(xnA, vt * T_TOK + n * 512, [[NT * T_TOK, 128], [1, 512]]),
                          start=(vt == 0), stop=(vt == 15)))
            WT("v", ssemn[sl], sfree[sl])
            W("v", "T")
            if sl == 0:
                S("v", "V", 1, lambda e, stg=stg: e.tensor_copy(stg[:], ps4a[:]))
            else:
                S("v", "V", 1, lambda e, stg=stg: e.tensor_copy(stg[:, 0:1024], ps2a[:]))
                S("v", "V", 1, lambda e, stg=stg: e.tensor_copy(stg[:, 1024:2048], ps2b[:]))
            W("y", "V")
            sfree[sl] = S("y", ssemn[sl], 16, lambda e, tt=tt, stg=stg: e.dma_start(
                sat_sp[tt, :, :], stg[:]))

        p3_T = C["T"]

        if DEBUG:
            W("y", "SP0", "SP1")
            S("y", "YC", 16, lambda e: e.dma_start(
                dbg["sat"][:, :, :], sat_sp[:, :, :]))

        tag("P4")
        wg_free = [0, 0]
        wu_free = [0, 0]
        wd_free = [0, 0]
        wgs_use = [0, 0]
        wus_use = [0, 0]
        wds_use = [0, 0]
        gu_T = 0
        cx_free = 0
        cs_free = 0
        co_free = 0
        W("y", "XQ0", "XQ1", "SP0", "SP1")
        W("g", "V")
        W("t", "SP0", "SP1")

        for tc in range(3):
            tw = TCH[tc]
            t0 = TOFF[tc]
            nmm = (tw + 511) // 512
            WT("y", "T", gu_T if tc > 0 else p3_T)
            S("y", "Q4", 16, lambda e, t0=t0, tw=tw: e.dma_start(
                bass.AP(oaA, XQC_OFF, [[NT * T_TOK, 128], [PSL, 16], [1, tw]]),
                xq_sp[:, :, t0:t0 + tw].rearrange("a p b -> p a b")))

            tag(f"P4.gu{tc}")
            for ht in range(NHT):
                sl = ht % 2
                psg = ps4a[:, sl * 1024: sl * 1024 + tw]
                psu = ps2a if sl == 0 else ps2b
                gsl = gsil if sl == 0 else gsilB
                if ht == 0 and tc == 0:
                    W("y", "PP")
                WT("y", "T", wgs_use[sl])
                S("y", ["WG0", "WG1"][sl], 16, lambda e, ht=ht, sl=sl: e.dma_start(
                    r3(wgl[sl][:], 128),
                    wgb[ht, :, :, :].rearrange("a p b -> p a b")))
                WT("y", "T", wus_use[sl])
                S("y", ["WU0", "WU1"][sl], 16, lambda e, ht=ht, sl=sl: e.dma_start(
                    r3(wul[sl][:], 128),
                    wub[ht, :, :, :].rearrange("a p b -> p a b")))
                W("s", ["WG0", "WG1"][sl])
                WT("s", "T", wgs_use[sl])
                sg = S("s", "S", 1, lambda e, sl=sl: e.sign(wgl[sl][:], wgl[sl][:]))
                W("s", ["WU0", "WU1"][sl])
                WT("s", "T", wus_use[sl])
                su = S("s", "S", 1, lambda e, sl=sl: e.sign(wul[sl][:], wul[sl][:]))
                WT("t", "S", su)
                if ht == 0:
                    W("t", "Q4")
                need("t", "p4g%d" % sl, "p4d%d" % sl, "ps2a" if sl == 0 else "ps2b")
                for dt in range(NT):
                    for n in range(nmm):
                        w0 = n * 512
                        w1 = min(tw, w0 + 512)
                        S("t", None, 1, lambda e, dt=dt, w0=w0, w1=w1, sl=sl, psg=psg: e.matmul(
                            psg[:, w0:w1],
                            wgl[sl][:, dt * 128:(dt + 1) * 128],
                            bass.AP(oaA, XQC_OFF + dt * PSL + w0,
                                    [[NT * T_TOK, 128], [1, w1 - w0]]),
                            start=(dt == 0), stop=(dt == NT - 1)))
                        S("t", "T" if (dt == NT - 1 and n == nmm - 1) else None, 1,
                          lambda e, dt=dt, w0=w0, w1=w1, sl=sl, psu=psu: e.matmul(
                              psu[:, w0:w1],
                              wul[sl][:, dt * 128:(dt + 1) * 128],
                              bass.AP(oaA, XQC_OFF + dt * PSL + w0,
                                      [[NT * T_TOK, 128], [1, w1 - w0]]),
                              start=(dt == 0), stop=(dt == NT - 1)))
                wgs_use[sl] = C["T"]
                wus_use[sl] = C["T"]
                W("s", "T")
                need("s", "gsl%d" % sl)
                S("s", "S", 1, lambda e, tw=tw, psg=psg, gsl=gsl: e.activation(
                    gsl[:, 0:tw], psg[:, 0:tw], AF.Silu))
                done("p4g%d" % sl, "S")
                sil = C["S"]
                WT("v", "S", sil)
                S("v", "V", 1, lambda e, ht=ht, tw=tw, psu=psu, gsl=gsl: e.tensor_mul(
                    p_slab(ht)[:, 0:tw], psu[:, 0:tw], gsl[:, 0:tw]))
                done("ps2a" if sl == 0 else "ps2b", "V")
                done("gsl%d" % sl, "V")
            done("pfull", "V")
            gu_T = C["T"]

            tag(f"P4.dn{tc}")
            for dt in range(NT):
                psd = ps4a[:, (dt % 2) * 1024:(dt % 2) * 1024 + tw]
                if dt < 2:
                    need("t", "pfull", "p4g%d" % (dt % 2))
                need("t", "p4d%d" % (dt % 2))
                for ht in range(NHT):
                    hsl = (ht // 16) % 2
                    hidx = ht % 16
                    if hidx == 0:
                        if dt == 0 and ht == 0 and tc == 0:
                            W("g", "PP")
                        WT("g", "T", wds_use[hsl])
                        S("g", ["FD0", "FD1"][hsl], 16,
                          lambda e, dt=dt, ht=ht, hsl=hsl: e.dma_start(
                              r3(wdl[hsl][:], 128),
                              wdb[dt, ht:ht + 16, :, :].rearrange("a p b -> p a b")))
                        W("s", ["FD0", "FD1"][hsl])
                        WT("s", "T", wds_use[hsl])
                        sd = S("s", "S", 1, lambda e, hsl=hsl: e.sign(
                            wdl[hsl][:], wdl[hsl][:]))
                        wd_free[hsl] = sd
                        WT("t", "S", sd)
                    for n in range(nmm):
                        w0 = n * 512
                        w1 = min(tw, w0 + 512)
                        S("t", "T" if (hidx == 15 and n == nmm - 1) else None, 1,
                          lambda e, ht=ht, hidx=hidx, hsl=hsl, w0=w0, w1=w1, psd=psd: e.matmul(
                              psd[:, w0:w1],
                              wdl[hsl][:, hidx * 128:(hidx + 1) * 128],
                              p_slab(ht)[:, w0:w1],
                              start=(ht == 0), stop=(ht == NHT - 1)))
                    if hidx == 15:
                        wds_use[hsl] = C["T"]
                W("s", "T")
                S("s", "S", 1, lambda e, dt=dt, tw=tw, psd=psd: e.activation(
                    bass.AP(oaA, GF_OFF + dt * PSL, [[NT * T_TOK, 128], [1, tw]]),
                    psd, AF.Silu))
                done("p4d%d" % (dt % 2), "S")
                done("gf%d" % dt, "S")

            tag(f"P4.cmb{tc}")
            ntt = tw // 128
            for ttl in range(ntt):
                tt = (t0 // 128) + ttl
                WT("y", "V", cx_free)
                S("y", "CX0", 16, lambda e, tt=tt: e.dma_start(
                    f32a[:], x_d[tt * 128:(tt + 1) * 128, :]))
                WT("y", "V", cs_free)
                S("y", "CS0", 16, lambda e, tt=tt: e.dma_start(
                    oa(SAT_OFF, 2048), sat_sp[tt, :, :]))
                for grp in range(4):
                    pp = ps2a if grp % 2 == 0 else ps2b
                    ppres = "ps2a" if grp % 2 == 0 else "ps2b"
                    need("t", ppres)
                    for j in range(4):
                        dt = grp * 4 + j
                        need("t", "gf%d" % dt)
                        S("t", "T" if j == 3 else None, 1,
                          lambda e, dt=dt, j=j, pp=pp, ttl=ttl: e.matmul(
                              pp[:, j * 128:(j + 1) * 128],
                              bass.AP(oaA, GF_OFF + dt * PSL + ttl * 128,
                                      [[NT * T_TOK, 128], [1, 128]]),
                              cIb[:], start=True, stop=True))
                    W("v", "T")
                    S("v", "V", 1, lambda e, grp=grp, pp=pp: e.tensor_copy(
                        oa(GT_OFF + grp * 512, 512), pp[:, 0:512]))
                    done(ppres, "V")
                W("v", "CX0", "CS0")
                WT("v", "CO0", co_free)
                cs_free = S("v", "V", 1, lambda e: e.tensor_mul(
                    okn[:], oa(GT_OFF, 2048), oa(SAT_OFF, 2048)))
                cx_free = S("v", "V", 1, lambda e: e.tensor_add(
                    f32b[:], okn[:], f32a[:]))
                W("y", "V")
                co_free = S("y", "CO0", 16, lambda e, tt=tt: e.dma_start(
                    out_d[tt * 128:(tt + 1) * 128, :], f32b[:]))

        W("y", "CO0")
        if DEBUG:
            W("y", "YC")

    with es, nc.Block() as block:
        @block.tensor
        def _(t):
            prog("t", t)

        @block.vector
        def _(v):
            prog("v", v)

        @block.scalar
        def _(s):
            prog("s", s)

        @block.gpsimd
        def _(g):
            prog("g", g)

        @block.sync
        def _(y):
            prog("y", y)

    return nc


def host_pack(inputs):
    x = np.ascontiguousarray(np.asarray(inputs["L0_attn_hidden_states"], dtype=np.float32))
    Wq = np.asarray(inputs["L0_attn_Wq"])
    Wk = np.asarray(inputs["L0_attn_Wk"])
    Wv = np.asarray(inputs["L0_attn_Wv"])
    Wg = np.asarray(inputs["L0_attn_Wg"])
    Ws = np.asarray(inputs["L0_attn_Ws"])
    Wo = np.asarray(inputs["L0_attn_Wo"])
    gnw = np.ascontiguousarray(np.asarray(inputs["L0_attn_g_norm_weight"], dtype=np.float32).reshape(128, 1))
    Wgf = np.asarray(inputs["L0_ffn_W_g"])
    Wuf = np.asarray(inputs["L0_ffn_W_u"])
    Wdf = np.asarray(inputs["L0_ffn_W_d"])

    fts = []
    for p in range(8):
        h0, h1 = 2 * p, 2 * p + 1
        fts.append(np.concatenate([Wq[:, h0 * 64:(h0 + 1) * 64], Wq[:, h1 * 64:(h1 + 1) * 64]], 1))
        fts.append(np.concatenate([Wk[:, h0 * 64:(h0 + 1) * 64], Wk[:, h1 * 64:(h1 + 1) * 64]], 1))
        fts.append(np.concatenate([Ws[:, h0 * 64:(h0 + 1) * 64], Ws[:, h1 * 64:(h1 + 1) * 64]], 1))
        fts.append(Wv[:, h0 * 128:(h0 + 1) * 128])
        fts.append(Wv[:, h1 * 128:(h1 + 1) * 128])
        fts.append(Wg[:, h0 * 128:(h0 + 1) * 128])
        fts.append(Wg[:, h1 * 128:(h1 + 1) * 128])
    attn_w = np.ascontiguousarray(np.stack(fts).reshape(56, 16, 128, 128))
    wo_w = np.ascontiguousarray(Wo.reshape(16, 128, D))
    wg_s = np.ascontiguousarray(Wgf.reshape(64, 128, 16, 128).transpose(0, 2, 3, 1))
    wu_s = np.ascontiguousarray(Wuf.reshape(64, 128, 16, 128).transpose(0, 2, 3, 1))
    wd_s = np.ascontiguousarray(Wdf.reshape(16, 128, 64, 128).transpose(0, 2, 3, 1))
    ident_bf = np.eye(128, dtype=ml_dtypes.bfloat16)
    ident_f32 = np.eye(128, dtype=np.float32)
    triu_bf = np.triu(np.ones((128, 128), np.float32)).astype(ml_dtypes.bfloat16)

    shared = dict(attn_w=attn_w, wo_w=wo_w, wg_s=wg_s, wu_s=wu_s, wd_s=wd_s,
                  gnw=gnw, ident_bf=ident_bf, ident_f32=ident_f32,
                  triu_bf=triu_bf)
    return x, shared


def kernel(**inputs):
    x, shared = host_pack(inputs)
    nc = build_nc()
    in_maps = [dict(shared, x=np.ascontiguousarray(x[c])) for c in range(8)]
    res = run_bass_kernel_spmd(nc, in_maps, core_ids=list(range(8)))
    return np.stack([np.asarray(res.results[c]["out"]) for c in range(8)], axis=0)


# version 04
# speedup vs baseline: 1.0305x; 1.0305x over previous
import numpy as np
import ml_dtypes
import concourse.bass as bass
import concourse.mybir as mybir
from concourse.bass_utils import run_bass_kernel_spmd

BF = mybir.dt.bfloat16
F32 = mybir.dt.float32
I32 = mybir.dt.int32
U8 = mybir.dt.uint8
AF = mybir.ActivationFunctionType
ALU = mybir.AluOpType
AX = mybir.AxisListType

T_TOK = 2048
D = 2048
NT = 16
NP = 8
NCH = 16
NHT = 64
TCH = [768, 768, 512]
TOFF = [0, 768, 1536]
PSL = 768

DEBUG = False
TAGS = []


def build_nc():
    nc = bass.Bass("TRN2", target_bir_lowering=False,
                   detect_race_conditions=False)

    x_d = nc.dram_tensor("x", [T_TOK, D], F32, kind="ExternalInput")
    aw_d = nc.dram_tensor("attn_w", [56, 16, 128, 128], F32, kind="ExternalInput")
    wo_d = nc.dram_tensor("wo_w", [16, 128, D], F32, kind="ExternalInput")
    wg_d = nc.dram_tensor("wg_s", [64, 16, 128, 128], F32, kind="ExternalInput")
    wu_d = nc.dram_tensor("wu_s", [64, 16, 128, 128], F32, kind="ExternalInput")
    wd_d = nc.dram_tensor("wd_s", [16, 64, 128, 128], F32, kind="ExternalInput")
    gnw_d = nc.dram_tensor("gnw", [128, 1], F32, kind="ExternalInput")
    ib_d = nc.dram_tensor("ident_bf", [128, 128], BF, kind="ExternalInput")
    if_d = nc.dram_tensor("ident_f32", [128, 128], F32, kind="ExternalInput")
    tri_d = nc.dram_tensor("triu_bf", [128, 128], BF, kind="ExternalInput")
    out_d = nc.dram_tensor("out", [T_TOK, D], F32, kind="ExternalOutput")
    xq_sp = nc.dram_tensor("xq_sp", [NT, 128, T_TOK], BF, kind="Internal")
    wgb = nc.dram_tensor("wgb", [64, 16, 128, 128], BF, kind="Internal")
    wub = nc.dram_tensor("wub", [64, 16, 128, 128], BF, kind="Internal")
    wdb = nc.dram_tensor("wdb", [16, 64, 128, 128], BF, kind="Internal")
    sat_sp = nc.dram_tensor("sat_sp", [NT, 128, D], BF, kind="Internal")

    dbg = {}
    if DEBUG:
        dbg["xnT"] = nc.dram_tensor("dbg_xnT", [NT, 128, T_TOK], BF, kind="ExternalOutput")
        dbg["xq"] = nc.dram_tensor("dbg_xq", [NT, 128, T_TOK], BF, kind="ExternalOutput")
        dbg["oall"] = nc.dram_tensor("dbg_oall", [16, 128, T_TOK], BF, kind="ExternalOutput")
        dbg["sat"] = nc.dram_tensor("dbg_sat", [NT, 128, D], BF, kind="ExternalOutput")
        dbg["st"] = nc.dram_tensor("dbg_st", [128, 6 * NT], F32, kind="ExternalOutput")

    from contextlib import ExitStack
    es = ExitStack()

    def sbuf(name, shape, dt):
        return es.enter_context(nc.sbuf_tensor(name, shape, dt))

    xnA = sbuf("xnA", [128, NT * T_TOK], BF)
    oaA = sbuf("oaA", [128, NT * T_TOK], BF)
    GF_OFF = 0
    XQC_OFF = 12288
    GT_OFF = 24576
    SAT_OFF = 26624
    POV_OFF = 28672

    cIb = sbuf("cIb", [128, 128], BF)
    cIf = sbuf("cIf", [128, 128], F32)
    cTri = sbuf("cTri", [128, 128], BF)
    cGnw = sbuf("cGnw", [128, 1], F32)
    cZ = sbuf("cZ", [128, 1], F32)

    stats = {}
    for nm in ("s1", "s2", "m", "r", "nb", "s2q", "amax", "r2", "sq", "inv",
               "sq2r", "t1"):
        stats[nm] = sbuf("st_" + nm, [128, NT], F32)
    smx = sbuf("smx", [128, 2 * NCH], F32)
    ssm = sbuf("ssm", [128, 2 * NCH], F32)
    srs = sbuf("srs", [128, 2 * NCH], F32)
    sos = sbuf("sos", [128, 2 * NCH], F32)
    sor = sbuf("sor", [128, 2 * NCH], F32)

    f32a = sbuf("f32a", [128, 2048], F32)
    f32b = sbuf("f32b", [128, 2048], F32)
    okn = sbuf("okn", [128, 2048], F32)

    AR = 54272
    arena = sbuf("arena", [128, AR], U8)
    ar_base = nc.lookup_mloc(arena).addr

    def at(name, off, shape, dt):
        return nc.alloc_sbuf_tensor_at(name, shape, dt, offset=ar_base + off)

    qqT = at("qqT", 0, [128, 2048], BF)
    kkpw = at("kkpw", 4096, [128, 2048], BF)
    embf = at("embf", 8192, [128, 2048], BF)
    etok = at("etok", 12288, [128, 2048], BF)
    ktok = at("ktok", 16384, [128, 2048], BF)
    vt0 = at("vt0", 20480, [128, 2048], BF)
    vt1 = at("vt1", 24576, [128, 2048], BF)
    gs0 = at("gs0", 28672, [128, 2048], BF)
    gs1 = at("gs1", 32768, [128, 2048], BF)
    sm0 = at("sm0", 36864, [128, 2048], BF)
    sm1 = at("sm1", 40960, [128, 2048], BF)
    pwT = at("pwT", 45056, [128, 2048], BF)
    scr4 = at("scr4", 49152, [128, 1024], F32)
    xqi = at("xqi", 20480, [128, 2048], I32)
    pext = at("pext", 0, [128, 17 * PSL], BF)
    wgl = [at("wgl0", 26624, [128, 2048], BF), at("wgl1", 30720, [128, 2048], BF)]
    wul = [at("wul0", 34816, [128, 2048], BF), at("wul1", 38912, [128, 2048], BF)]
    wdl = [at("wdl0", 43008, [128, 2048], BF), at("wdl1", 47104, [128, 2048], BF)]
    gsil = at("gsil", 51200, [128, 768], BF)
    gsilB = at("gsilB", 52736, [128, 768], BF)

    ps4a = es.enter_context(nc.psum_tensor("ps4a", [128, 2048], F32))
    ps2a = es.enter_context(nc.psum_tensor("ps2a", [128, 1024], F32))
    ps2b = es.enter_context(nc.psum_tensor("ps2b", [128, 1024], F32))

    sem_names = ["T", "V", "S", "YC",
                 "XA", "XB", "XQ0", "XQ1",
                 "WA0", "WA1", "GWO", "SP0", "SP1",
                 "Q4", "WG0", "WG1", "WU0", "WU1", "FD0", "FD1", "PP",
                 "CX0", "CS0", "CO0"]
    sems = {n: nc.alloc_semaphore(name="sem_" + n) for n in sem_names}

    def zbc(n):
        return bass.AP(cZ, 0, [[1, 128], [0, n]])

    def tri_bc(nch):
        return bass.AP(cTri, 0, [[128, 128], [0, nch], [1, 128]])

    def r3(ap, b):
        return ap.rearrange("p (a b) -> p a b", b=b)

    def p_slab(ht):
        if ht < 42:
            return bass.AP(xnA, ht * PSL, [[NT * T_TOK, 128], [1, PSL]])
        if ht < 47:
            return bass.AP(oaA, POV_OFF + (ht - 42) * PSL, [[NT * T_TOK, 128], [1, PSL]])
        return bass.AP(pext, (ht - 47) * PSL, [[17 * PSL, 128], [1, PSL]])

    def oa(off, n):
        return bass.AP(oaA, off, [[NT * T_TOK, 128], [1, n]])

    def prog(me, eng):
        C = {k: 0 for k in sems}
        LW = {k: 0 for k in sems}
        _tmm = [0]
        rec = {}

        def tag(name):
            if me == "t":
                TAGS.append((name, _tmm[0]))

        def done(res, engsem):
            rec[res] = (engsem, C[engsem])

        def need(who, *ress):
            for res in ress:
                e_, c_ = rec.get(res, (None, 0))
                if e_ is not None:
                    WT(who, e_, c_)

        def W(who, *names):
            if who != me:
                return
            for n_ in names:
                if C[n_] > 0 and LW[n_] < C[n_]:
                    eng.wait_ge(sems[n_], C[n_])
                    LW[n_] = C[n_]

        def WT(who, sem, val):
            if who == me and val > 0 and LW[sem] < val:
                eng.wait_ge(sems[sem], val)
                LW[sem] = val

        def S(owner, sem, n, f):
            inst = f(eng) if owner == me else None
            if owner == "t":
                _tmm[0] += 1
            if sem is not None:
                C[sem] += n
                if inst is not None:
                    inst.then_inc(sems[sem], n)
            return C[sem] if sem is not None else None

        S("y", "YC", 16, lambda e: e.dma_start(cIb[:], ib_d[:, :]))
        S("y", "YC", 16, lambda e: e.dma_start(cIf[:], if_d[:, :]))
        S("y", "YC", 16, lambda e: e.dma_start(cTri[:], tri_d[:, :]))
        S("y", "YC", 16, lambda e: e.dma_start(cGnw[:], gnw_d[:, :]))
        W("v", "YC")
        S("v", "V", 1, lambda e: e.memset(cZ[:], 0.0))

        for k in range(8):
            S("g", "PP", 16, lambda e, k=k: e.dma_start(
                wgb[k * 8:(k + 1) * 8, :, :, :], wg_d[k * 8:(k + 1) * 8, :, :, :]))
            S("g", "PP", 16, lambda e, k=k: e.dma_start(
                wub[k * 8:(k + 1) * 8, :, :, :], wu_d[k * 8:(k + 1) * 8, :, :, :]))
            S("g", "PP", 16, lambda e, k=k: e.dma_start(
                wdb[k * 2:(k + 1) * 2, :, :, :], wd_d[k * 2:(k + 1) * 2, :, :, :]))

        tag("P1")
        xslot = [f32a, f32b]
        xsem = ["XA", "XB"]
        xfree = [0, 0]
        xfree_s = [0, 0]
        qstg = [etok, ktok]
        qsem = ["XQ0", "XQ1"]
        qfree = [0, 0]

        def stv(name, f):
            W("v", "V", "S")
            S("v", "V", 1, f)

        for i in range(NT):
            sl = i % 2
            xt = xslot[sl]
            WT("y", "V", xfree[sl])
            WT("y", "S", xfree_s[sl])
            S("y", xsem[sl], 16, lambda e, i=i, xt=xt: e.dma_start(
                xt[:], x_d[i * 128:(i + 1) * 128, :]))
            W("s", xsem[sl])
            xfree_s[sl] = S("s", "S", 1, lambda e, i=i, xt=xt: e.activation(
                oa(15 * T_TOK, 2048), xt[:], AF.Square,
                accum_out=stats["s2"][:, i:i + 1]))
            W("v", xsem[sl])
            xfree[sl] = S("v", "V", 1, lambda e, i=i, xt=xt: e.reduce_sum(
                stats["s1"][:, i:i + 1], xt[:], axis=AX.X))
        W("v", "S")
        stv("m", lambda e: e.tensor_scalar_mul(stats["m"][:, :], stats["s1"][:, :], 1.0 / D))
        stv("t1", lambda e: e.tensor_mul(stats["t1"][:, :], stats["m"][:, :], stats["m"][:, :]))
        stv("v1", lambda e: e.tensor_scalar_mul(stats["r"][:, :], stats["s2"][:, :], 1.0 / D))
        stv("v2", lambda e: e.tensor_sub(stats["r"][:, :], stats["r"][:, :], stats["t1"][:, :]))
        stv("v3", lambda e: e.tensor_scalar_add(stats["r"][:, :], stats["r"][:, :], 1e-5))
        W("s", "V")
        S("s", "S", 1, lambda e: e.activation(stats["r"][:, :], stats["r"][:, :], AF.Sqrt))
        W("v", "S")
        stv("v4", lambda e: e.reciprocal(stats["r"][:, :], stats["r"][:, :]))
        stv("v5", lambda e: e.tensor_mul(stats["nb"][:, :], stats["m"][:, :], stats["r"][:, :]))
        stv("v6", lambda e: e.tensor_scalar_mul(stats["nb"][:, :], stats["nb"][:, :], -1.0))
        done("LN", "V")

        for i in range(NT):
            sl = i % 2
            xt = xslot[sl]
            WT("y", "V", xfree[sl])
            WT("y", "S", xfree_s[sl])
            S("y", xsem[sl], 16, lambda e, i=i, xt=xt: e.dma_start(
                xt[:], x_d[i * 128:(i + 1) * 128, :]))
            W("v", xsem[sl])
            need("v", "LN")
            W("v", "S")
            S("v", "V", 1, lambda e, i=i, xt=xt: e.scalar_tensor_tensor(
                okn[:], xt[:], stats["r"][:, i:i + 1],
                bass.AP(stats["nb"], i, [[NT, 128], [0, 2048]]),
                ALU.mult, ALU.add))
            xfree[sl] = C["V"]
            xfree_s[sl] = 0
            W("s", "V")
            S("s", "S", 1, lambda e, i=i: e.activation(
                oa(15 * T_TOK, 2048), okn[:], AF.Square,
                accum_out=stats["s2q"][:, i:i + 1]))
            S("v", "V", 1, lambda e, i=i: e.reduce_max(
                stats["amax"][:, i:i + 1], okn[:], axis=AX.X,
                apply_absolute_value=True))
            W("v", "T")
            S("v", "V", 1, lambda e: e.tensor_copy(sm0[:], okn[:]))
            W("t", "YC")
            for grp in range(4):
                pp = ps2a if grp % 2 == 0 else ps2b
                W("t", "V", "S")
                for j in range(4):
                    dt = grp * 4 + j
                    S("t", "T" if j == 3 else None, 1, lambda e, dt=dt, j=j, pp=pp: e.matmul(
                        pp[:, j * 128:(j + 1) * 128],
                        sm0[:, dt * 128:(dt + 1) * 128], cIb[:],
                        start=True, stop=True))
                W("s", "T")
                S("s", "S", 1, lambda e, grp=grp, pp=pp, i=i: e.copy(
                    bass.AP(xnA, (grp * 4) * T_TOK + i * 128,
                            [[NT * T_TOK, 128], [T_TOK, 4], [1, 128]]),
                    pp[:, 0:512]))
            done("xnT%d" % i, "S")
        W("v", "S")
        stv("q1", lambda e: e.tensor_scalar(stats["r2"][:, :], stats["s2q"][:, :],
                                            1.0 / D, 1e-8, ALU.mult, ALU.add))
        W("s", "V")
        S("s", "S", 1, lambda e: e.activation(stats["r2"][:, :], stats["r2"][:, :], AF.Sqrt))
        W("v", "S")
        stv("q2", lambda e: e.reciprocal(stats["r2"][:, :], stats["r2"][:, :]))
        stv("q3", lambda e: e.tensor_mul(stats["amax"][:, :], stats["amax"][:, :], stats["r2"][:, :]))
        stv("q4", lambda e: e.reciprocal(stats["sq"][:, :], stats["amax"][:, :]))
        stv("q5", lambda e: e.tensor_scalar_mul(stats["sq"][:, :], stats["sq"][:, :], 127.0))
        stv("q6", lambda e: e.reciprocal(stats["inv"][:, :], stats["sq"][:, :]))
        stv("q7", lambda e: e.tensor_mul(stats["sq2r"][:, :], stats["sq"][:, :], stats["r2"][:, :]))
        stv("q8", lambda e: e.tensor_mul(stats["t1"][:, :], stats["r"][:, :], stats["sq2r"][:, :]))
        stv("q9", lambda e: e.tensor_mul(stats["s1"][:, :], stats["nb"][:, :], stats["sq2r"][:, :]))
        done("QS", "V")

        for i in range(NT):
            sl = i % 2
            xt = xslot[sl]
            WT("y", "V", xfree[sl])
            S("y", xsem[sl], 16, lambda e, i=i, xt=xt: e.dma_start(
                xt[:], x_d[i * 128:(i + 1) * 128, :]))
            W("v", xsem[sl])
            need("v", "QS")
            S("v", "V", 1, lambda e, i=i, xt=xt: e.scalar_tensor_tensor(
                okn[:], xt[:], stats["t1"][:, i:i + 1],
                bass.AP(stats["s1"], i, [[NT, 128], [0, 2048]]),
                ALU.mult, ALU.add))
            xfree[sl] = C["V"]
            S("v", "V", 1, lambda e: e.tensor_copy(xqi[:], okn[:]))
            W("v", "V", "T")
            S("v", "V", 1, lambda e, i=i: e.tensor_scalar_mul(
                sm1[:], xqi[:], stats["inv"][:, i:i + 1]))
            qsl = i % 2
            stg = qstg[qsl]
            for grp in range(4):
                pp = ps2a if grp % 2 == 0 else ps2b
                W("t", "V", "S")
                WT("t", qsem[qsl], qfree[qsl])
                for j in range(4):
                    dt = grp * 4 + j
                    S("t", "T" if j == 3 else None, 1, lambda e, dt=dt, j=j, pp=pp: e.matmul(
                        pp[:, j * 128:(j + 1) * 128],
                        sm1[:, dt * 128:(dt + 1) * 128], cIb[:],
                        start=True, stop=True))
                W("s", "T")
                S("s", "S", 1, lambda e, grp=grp, pp=pp, stg=stg: e.copy(
                    stg[:, grp * 512:(grp + 1) * 512], pp[:, 0:512]))
            W("y", "S")
            qfree[qsl] = S("y", qsem[qsl], 16, lambda e, i=i, stg=stg: e.dma_start(
                xq_sp[:, :, i * 128:(i + 1) * 128].rearrange("a p b -> p a b"),
                r3(stg[:], 128)))

        if DEBUG:
            W("y", "S")
            S("y", "YC", 16, lambda e: e.dma_start(
                dbg["xnT"][:, :, :].rearrange("a p b -> p a b"),
                r3(xnA[:], T_TOK)))
            W("y", "XQ0", "XQ1")
            S("y", "YC", 16, lambda e: e.dma_start(
                dbg["xq"][:, :, :], xq_sp[:, :, :]))

        tag("P2")
        wfree = [0, 0]

        def wp_ap(p, par, tile=None):
            off = (2 * p + par) * T_TOK
            if tile is None:
                return bass.AP(oaA, off, [[NT * T_TOK, 128], [1, 2048]])
            return bass.AP(oaA, off + tile * 128, [[NT * T_TOK, 128], [1, 128]])

        def tgroup(src, dst, fp32, src_res=None, dst_res=None, who_evac="s"):
            esem = "V" if who_evac == "v" else "S"
            for grp in range(4):
                pp = ps2a if grp % 2 == 0 else ps2b
                ppres = "ps2a" if grp % 2 == 0 else "ps2b"
                if src_res:
                    need("t", src_res)
                need("t", ppres)
                for j in range(4):
                    c = grp * 4 + j
                    if fp32:
                        S("t", "T" if j == 3 else None, 1,
                          lambda e, c=c, j=j, pp=pp, src=src: e.transpose(
                              pp[:, j * 128:(j + 1) * 128],
                              src[:, c * 128:(c + 1) * 128], cIf[:]))
                    else:
                        S("t", "T" if j == 3 else None, 1,
                          lambda e, c=c, j=j, pp=pp, src=src: e.matmul(
                              pp[:, j * 128:(j + 1) * 128],
                              src[:, c * 128:(c + 1) * 128], cIb[:],
                              start=True, stop=True))
                W(who_evac, "T")
                if who_evac == "s":
                    S("s", "S", 1, lambda e, grp=grp, pp=pp, dst=dst: e.copy(
                        dst[:, grp * 512:(grp + 1) * 512], pp[:, 0:512]))
                else:
                    S("v", "V", 1, lambda e, grp=grp, pp=pp, dst=dst: e.tensor_copy(
                        dst[:, grp * 512:(grp + 1) * 512], pp[:, 0:512]))
                done(ppres, esem)
            if dst_res:
                done(dst_res, esem)

        def wp_load(p, f):
            par = f % 2
            WT("g", "T", wfree[par])
            S("g", ["WA0", "WA1"][par], 16, lambda e, p=p, f=f, par=par: e.dma_start(
                r3(wp_ap(p, par), 128),
                aw_d[p * 7 + f, :, :, :].rearrange("a p b -> p a b")))

        for p in range(NP):
            tag(f"P2.proj")
            wfree[0] = wfree[1] = 0
            wp_load(p, 0)
            wp_load(p, 1)
            for f in range(7):
                par = f % 2
                W("t", ["WA0", "WA1"][par])
                for n in range(4):
                    if p == 0:
                        need("t", "xnT%d" % (4 * n + 3))
                    need("t", "psA%d" % n)
                    for dt in range(NT):
                        S("t", "T" if dt == NT - 1 else None, 1,
                          lambda e, n=n, dt=dt, p=p, par=par: e.matmul(
                              ps4a[:, n * 512:(n + 1) * 512],
                              wp_ap(p, par, dt),
                              bass.AP(xnA, dt * T_TOK + n * 512, [[NT * T_TOK, 128], [1, 512]]),
                              start=(dt == 0), stop=(dt == NT - 1)))
                    if n == 3:
                        wfree[par] = C["T"]
                        if f + 2 < 7:
                            wp_load(p, f + 2)
                    pc = ps4a[:, n * 512:(n + 1) * 512]
                    if f == 2:
                        W("v", "T")
                        S("v", "V", 1, lambda e, n=n, pc=pc: e.tensor_scalar(
                            f32a[:, n * 512:(n + 1) * 512], pc,
                            -32.0, 32.0, ALU.max, ALU.min))
                        done("psA%d" % n, "V")
                    else:
                        W("s", "T")
                        if f == 0:
                            S("s", "S", 1, lambda e, n=n, pc=pc: e.mul(
                                qqT[:, n * 512:(n + 1) * 512], pc, 0.125))
                        elif f == 1:
                            S("s", "S", 1, lambda e, n=n, pc=pc: e.copy(
                                kkpw[:, n * 512:(n + 1) * 512], pc))
                        elif f in (3, 4):
                            vtb = vt0 if f == 3 else vt1
                            S("s", "S", 1, lambda e, n=n, pc=pc, vtb=vtb: e.copy(
                                vtb[:, n * 512:(n + 1) * 512], pc))
                        else:
                            gsb = gs0 if f == 5 else gs1
                            S("s", "S", 1, lambda e, n=n, pc=pc, gsb=gsb: e.activation(
                                gsb[:, n * 512:(n + 1) * 512], pc, AF.Silu))
                        done("psA%d" % n, "S")
                if f == 0:
                    done("qqT", "S")
                elif f == 1:
                    done("kkT", "S")
                elif f == 2:
                    done("sclip", "V")
                elif f == 3:
                    done("vt0", "S")
                elif f == 4:
                    done("vt1", "S")
                elif f == 5:
                    done("gs0", "S")
                elif f == 6:
                    done("gs1", "S")
                if f == 2:
                    need("s", "sclip")
                    S("s", "S", 1, lambda e: e.activation(f32b[:], f32a[:], AF.Exp))
                    done("E32", "S")
                    need("v", "E32")
                    S("v", "V", 1, lambda e: e.tensor_tensor_scan(
                        f32a[:], f32b[:], zbc(2048), 0.0, ALU.add, ALU.add))
                    S("v", "V", 1, lambda e: e.tensor_copy(embf[:], f32b[:]))
                    done("embf", "V")
                    S("v", "V", 1, lambda e: e.reciprocal(f32b[:], f32a[:]))
                    done("recipAm", "V")

            tag("P2.tr")
            if p == 0:
                W("s", "XQ0", "XQ1")
            tgroup(embf, etok, False, "embf", "etok")
            tgroup(kkpw, ktok, False, "kkT", "Ktok")
            for vtb, stg, rr in ((vt0, sm0, "vt0"), (vt1, sm1, "vt1")):
                tgroup(vtb, stg, False, rr, rr + "st")
                need("v", rr + "st")
                S("v", "V", 1, lambda e, vtb=vtb, stg=stg: e.tensor_copy(vtb[:], stg[:]))
                done(rr + "tok", "V")
            tgroup(f32b, f32a, True, "recipAm", "recipA")

            tag("P2.pk")
            for rnd in range(2):
                need("t", "Ktok", "etok", "ps2b")
                if rnd == 1:
                    W("t", "V")
                for cc in range(8):
                    c = rnd * 8 + cc
                    S("t", "T" if cc == 7 else None, 1, lambda e, c=c, cc=cc: e.matmul(
                        ps2b[:, cc * 128:(cc + 1) * 128],
                        ktok[:, c * 128:(c + 1) * 128],
                        etok[:, c * 128:(c + 1) * 128],
                        start=True, stop=True))
                W("v", "T")
                if rnd == 0:
                    S("v", "V", 1, lambda e: e.memset(f32b[:, 0:128], 0.0))
                for cc in range(8):
                    c = rnd * 8 + cc
                    if c == NCH - 1:
                        break
                    S("v", "V", 1, lambda e, c=c, cc=cc: e.tensor_add(
                        f32b[:, (c + 1) * 128:(c + 2) * 128],
                        f32b[:, c * 128:(c + 1) * 128],
                        ps2b[:, cc * 128:(cc + 1) * 128]))
            S("v", "V", 1, lambda e: e.tensor_copy(ktok[:], f32b[:]))
            done("Hk", "V")
            done("ps2b", "V")

            tag("P2.sok")
            for h in range(2):
                smh = sm0 if h == 0 else sm1
                pso = ps2a if h == 0 else ps2b
                okh = okn[:, h * 1024:(h + 1) * 1024]
                for c in range(NCH):
                    if c % 4 == 0:
                        need("t", "kkT", "qqT", "psA%d" % (c // 4))
                    S("t", "T" if c % 4 == 3 else None, 1, lambda e, c=c, h=h: e.matmul(
                        ps4a[:, c * 128:(c + 1) * 128],
                        kkpw[h * 64:(h + 1) * 64, c * 128:(c + 1) * 128],
                        qqT[h * 64:(h + 1) * 64, c * 128:(c + 1) * 128],
                        start=True, stop=True))
                    if c % 4 == 3:
                        W("v", "T")
                        g4 = c // 4
                        S("v", "V", 1, lambda e, g4=g4, smh=smh: e.tensor_tensor(
                            r3(smh[:, g4 * 512:(g4 + 1) * 512], 128),
                            r3(ps4a[:, g4 * 512:(g4 + 1) * 512], 128),
                            tri_bc(4), ALU.mult))
                        done("psA%d" % g4, "V")
                        done("smm%d_%d" % (h, g4), "V")
                need("t", "etok", "Hk", "ps2a" if h == 0 else "ps2b")
                for c in range(NCH):
                    last = (c == NCH - 1)
                    if c % 4 == 0:
                        need("t", "smm%d_%d" % (h, c // 4))
                    S("t", None, 1, lambda e, c=c, h=h, smh=smh, pso=pso: e.matmul(
                        pso[:, c * 64:(c + 1) * 64],
                        smh[:, c * 128:(c + 1) * 128],
                        etok[:, c * 128 + h * 64: c * 128 + h * 64 + 64],
                        start=True, stop=(c == 0)))
                    if c > 0:
                        S("t", "T" if last else None, 1, lambda e, c=c, h=h: e.matmul(
                            pso[:, c * 64:(c + 1) * 64],
                            qqT[h * 64:(h + 1) * 64, c * 128:(c + 1) * 128],
                            ktok[h * 64:(h + 1) * 64, c * 128 + h * 64: c * 128 + h * 64 + 64],
                            start=False, stop=True))
                W("v", "T")
                need("v", "recipA")
                S("v", "V", 1, lambda e, okh=okh, h=h, pso=pso: e.tensor_tensor(
                    r3(okh, 64), r3(pso[:], 64),
                    bass.AP(f32a, h * 64, [[2048, 128], [128, NCH], [1, 64]]),
                    ALU.mult))
                done("ps2a" if h == 0 else "ps2b", "V")
                S("v", "V", 1, lambda e, okh=okh, h=h: e.reduce_max(
                    smx[:, h * NCH:(h + 1) * NCH], r3(okh, 64),
                    axis=AX.X, negate=True))
                done("smx%d" % h, "V")
            for h in range(2):
                okh = okn[:, h * 1024:(h + 1) * 1024]
                need("s", "smx%d" % h)
                if h == 1:
                    W("s", "V")
                for c in range(NCH):
                    S("s", "S" if c == NCH - 1 else None, 1, lambda e, c=c, okh=okh, h=h: e.activation(
                        scr4[:, c * 64:(c + 1) * 64], okh[:, c * 64:(c + 1) * 64],
                        AF.Exp, bias=smx[:, h * NCH + c:h * NCH + c + 1],
                        accum_out=ssm[:, h * NCH + c:h * NCH + c + 1]))
                done("exp%d" % h, "S")
                need("v", "exp%d" % h)
                W("v", "T")
                W("v", "V")
                S("v", "V", 1, lambda e, h=h: e.reciprocal(
                    srs[:, h * NCH:(h + 1) * NCH], ssm[:, h * NCH:(h + 1) * NCH]))
                W("v", "V")
                S("v", "V", 1, lambda e, okh=okh, h=h: e.tensor_tensor(
                    r3(okh, 64),
                    bass.AP(f32a, h * 64, [[2048, 128], [128, NCH], [1, 64]]),
                    bass.AP(srs, h * NCH, [[2 * NCH, 128], [1, NCH], [0, 64]]),
                    ALU.mult))
                S("v", "V", 1, lambda e, okh=okh, h=h: e.tensor_tensor(
                    bass.AP(kkpw, h * 64, [[2048, 128], [128, NCH], [1, 64]]),
                    r3(scr4[:], 64), r3(okh, 64), ALU.mult))

            done("pw", "V")
            tag("P2.pw")
            tgroup(kkpw, pwT, False, "pw", "pwT")

            for rnd in range(2):
                need("t", "etok", "vt0tok", "vt1tok", "ps2b")
                W("t", "V")
                for cc in range(8):
                    c = rnd * 8 + cc
                    S("t", None, 1, lambda e, c=c, cc=cc: e.matmul(
                        ps2b[0:64, cc * 128:(cc + 1) * 128],
                        etok[:, c * 128: c * 128 + 64],
                        vt0[:, c * 128:(c + 1) * 128],
                        start=True, stop=True))
                    S("t", "T" if cc == 7 else None, 1, lambda e, c=c, cc=cc: e.matmul(
                        ps2b[64:128, cc * 128:(cc + 1) * 128],
                        etok[:, c * 128 + 64: c * 128 + 128],
                        vt1[:, c * 128:(c + 1) * 128],
                        start=True, stop=True))
                W("v", "T")
                if rnd == 0:
                    S("v", "V", 1, lambda e: e.memset(f32b[:, 0:128], 0.0))
                for cc in range(8):
                    c = rnd * 8 + cc
                    if c == NCH - 1:
                        break
                    S("v", "V", 1, lambda e, c=c, cc=cc: e.tensor_add(
                        f32b[:, (c + 1) * 128:(c + 2) * 128],
                        f32b[:, c * 128:(c + 1) * 128],
                        ps2b[:, cc * 128:(cc + 1) * 128]))
            S("v", "V", 1, lambda e: e.tensor_copy(qqT[:], f32b[:]))
            done("Hv", "V")
            done("ps2b", "V")

            tag("P2.ov")
            for h in range(2):
                smh = sm0 if h == 0 else sm1
                vth = vt0 if h == 0 else vt1
                gsh = gs0 if h == 0 else gs1
                hh = 2 * p + h
                for c in range(NCH):
                    if c % 4 == 0:
                        need("t", "embf", "pwT", "psA%d" % (c // 4))
                    S("t", "T" if c % 4 == 3 else None, 1, lambda e, c=c, h=h: e.matmul(
                        ps4a[:, c * 128:(c + 1) * 128],
                        embf[h * 64:(h + 1) * 64, c * 128:(c + 1) * 128],
                        pwT[h * 64:(h + 1) * 64, c * 128:(c + 1) * 128],
                        start=True, stop=True))
                    if c % 4 == 3:
                        W("v", "T")
                        g4 = c // 4
                        S("v", "V", 1, lambda e, g4=g4, smh=smh: e.tensor_tensor(
                            r3(smh[:, g4 * 512:(g4 + 1) * 512], 128),
                            r3(ps4a[:, g4 * 512:(g4 + 1) * 512], 128),
                            tri_bc(4), ALU.mult))
                        done("psA%d" % g4, "V")
                        done("smm%d_%d" % (h, g4), "V")
                need("t", "Hv", "vt%dtok" % h)
                for c in range(NCH):
                    last = (c == NCH - 1)
                    if c % 4 == 0:
                        need("t", "smm%d_%d" % (h, c // 4), "psA%d" % (c // 4))
                    S("t", None, 1, lambda e, c=c, smh=smh, vth=vth: e.matmul(
                        ps4a[:, c * 128:(c + 1) * 128],
                        smh[:, c * 128:(c + 1) * 128],
                        vth[:, c * 128:(c + 1) * 128],
                        start=True, stop=(c == 0)))
                    if c > 0:
                        S("t", "T" if last else None, 1, lambda e, c=c, h=h: e.matmul(
                            ps4a[:, c * 128:(c + 1) * 128],
                            pwT[h * 64:(h + 1) * 64, c * 128:(c + 1) * 128],
                            qqT[h * 64:(h + 1) * 64, c * 128:(c + 1) * 128],
                            start=False, stop=True))
                W("s", "T")
                need("s", "Hv")
                S("s", "S", 1, lambda e: e.activation(f32b[:], ps4a[:], AF.Square))
                W("v", "S")
                S("v", "V", 1, lambda e, h=h: e.reduce_sum(
                    sos[:, h * NCH:(h + 1) * NCH], r3(f32b[:], 128), axis=AX.X))
                W("v", "V")
                S("v", "V", 1, lambda e, h=h: e.tensor_scalar(
                    sor[:, h * NCH:(h + 1) * NCH], sos[:, h * NCH:(h + 1) * NCH],
                    1.0 / 128.0, 1e-5, ALU.mult, ALU.add))
                W("s", "V")
                S("s", "S", 1, lambda e, h=h: e.activation(
                    sor[:, h * NCH:(h + 1) * NCH], sor[:, h * NCH:(h + 1) * NCH],
                    AF.Sqrt))
                W("v", "S")
                W("v", "V")
                S("v", "V", 1, lambda e, h=h: e.reciprocal(
                    sor[:, h * NCH:(h + 1) * NCH], sor[:, h * NCH:(h + 1) * NCH]))
                W("v", "V")
                S("v", "V", 1, lambda e, h=h, smh=smh: e.tensor_tensor(
                    r3(smh[:], 128), r3(ps4a[:], 128),
                    bass.AP(sor, h * NCH, [[2 * NCH, 128], [1, NCH], [0, 128]]),
                    ALU.mult))
                for grp in range(4):
                    pp = ps2a if grp % 2 == 0 else ps2b
                    W("t", "V", "S")
                    for j in range(4):
                        c = grp * 4 + j
                        S("t", "T" if j == 3 else None, 1,
                          lambda e, c=c, j=j, pp=pp, smh=smh: e.matmul(
                              pp[:, j * 128:(j + 1) * 128],
                              smh[:, c * 128:(c + 1) * 128], cIb[:],
                              start=True, stop=True))
                    W("v", "T")
                    S("v", "V", 1, lambda e, grp=grp, pp=pp, gsh=gsh, hh=hh: e.scalar_tensor_tensor(
                        oa(hh * T_TOK + grp * 512, 512),
                        pp[:, 0:512], cGnw[:, 0:1],
                        gsh[:, grp * 512:(grp + 1) * 512],
                        ALU.mult, ALU.mult))

        p2_T = C["T"]

        if DEBUG:
            W("y", "V")
            S("y", "YC", 16, lambda e: e.dma_start(
                dbg["oall"][:, :, :].rearrange("a p b -> p a b"),
                r3(oaA[:], T_TOK)))

        tag("P3")
        WT("g", "T", p2_T)
        for vt in range(16):
            S("g", "GWO", 16, lambda e, vt=vt: e.dma_start(
                bass.AP(xnA, vt * T_TOK, [[NT * T_TOK, 128], [1, D]]),
                wo_d[vt, :, :]))
        sslot = [qqT, kkpw]
        ssemn = ["SP0", "SP1"]
        sfree = [0, 0]
        for tt in range(NT):
            sl = tt % 2
            stg = sslot[sl]
            W("t", "GWO")
            W("t", "V")
            for vt in range(16):
                for n in range(4):
                    if sl == 0:
                        tgt = ps4a[:, n * 512:(n + 1) * 512]
                    else:
                        tgt = (ps2a if n < 2 else ps2b)[:, (n % 2) * 512:((n % 2) + 1) * 512]
                    S("t", "T" if (vt == 15 and n == 3) else None, 1,
                      lambda e, vt=vt, n=n, tt=tt, tgt=tgt: e.matmul(
                          tgt,
                          oa(vt * T_TOK + tt * 128, 128),
                          bass.AP(xnA, vt * T_TOK + n * 512, [[NT * T_TOK, 128], [1, 512]]),
                          start=(vt == 0), stop=(vt == 15)))
            WT("v", ssemn[sl], sfree[sl])
            W("v", "T")
            if sl == 0:
                S("v", "V", 1, lambda e, stg=stg: e.tensor_copy(stg[:], ps4a[:]))
            else:
                S("v", "V", 1, lambda e, stg=stg: e.tensor_copy(stg[:, 0:1024], ps2a[:]))
                S("v", "V", 1, lambda e, stg=stg: e.tensor_copy(stg[:, 1024:2048], ps2b[:]))
            W("y", "V")
            sfree[sl] = S("y", ssemn[sl], 16, lambda e, tt=tt, stg=stg: e.dma_start(
                sat_sp[tt, :, :], stg[:]))

        p3_T = C["T"]

        if DEBUG:
            W("y", "SP0", "SP1")
            S("y", "YC", 16, lambda e: e.dma_start(
                dbg["sat"][:, :, :], sat_sp[:, :, :]))

        tag("P4")
        wg_free = [0, 0]
        wu_free = [0, 0]
        wd_free = [0, 0]
        wgs_use = [0, 0]
        wus_use = [0, 0]
        wds_use = [0, 0]
        gu_T = 0
        cx_free = 0
        cs_free = 0
        co_free = 0
        W("y", "XQ0", "XQ1", "SP0", "SP1")
        W("g", "V")
        W("t", "SP0", "SP1")

        for tc in range(3):
            tw = TCH[tc]
            t0 = TOFF[tc]
            nmm = (tw + 511) // 512
            WT("y", "T", gu_T if tc > 0 else p3_T)
            S("y", "Q4", 16, lambda e, t0=t0, tw=tw: e.dma_start(
                bass.AP(oaA, XQC_OFF, [[NT * T_TOK, 128], [PSL, 16], [1, tw]]),
                xq_sp[:, :, t0:t0 + tw].rearrange("a p b -> p a b")))

            tag(f"P4.gu{tc}")
            for ht in range(NHT):
                sl = ht % 2
                psg = ps4a[:, sl * 1024: sl * 1024 + tw]
                psu = ps2a if sl == 0 else ps2b
                gsl = gsil if sl == 0 else gsilB
                if ht == 0 and tc == 0:
                    W("y", "PP")
                WT("y", "T", wgs_use[sl])
                S("y", ["WG0", "WG1"][sl], 16, lambda e, ht=ht, sl=sl: e.dma_start(
                    r3(wgl[sl][:], 128),
                    wgb[ht, :, :, :].rearrange("a p b -> p a b")))
                WT("y", "T", wus_use[sl])
                S("y", ["WU0", "WU1"][sl], 16, lambda e, ht=ht, sl=sl: e.dma_start(
                    r3(wul[sl][:], 128),
                    wub[ht, :, :, :].rearrange("a p b -> p a b")))
                W("s", ["WG0", "WG1"][sl])
                WT("s", "T", wgs_use[sl])
                sg = S("s", "S", 1, lambda e, sl=sl: e.sign(wgl[sl][:], wgl[sl][:]))
                W("s", ["WU0", "WU1"][sl])
                WT("s", "T", wus_use[sl])
                su = S("s", "S", 1, lambda e, sl=sl: e.sign(wul[sl][:], wul[sl][:]))
                WT("t", "S", su)
                if ht == 0:
                    W("t", "Q4")
                need("t", "p4g%d" % sl, "p4d%d" % sl, "ps2a" if sl == 0 else "ps2b")
                for dt in range(NT):
                    for n in range(nmm):
                        w0 = n * 512
                        w1 = min(tw, w0 + 512)
                        S("t", None, 1, lambda e, dt=dt, w0=w0, w1=w1, sl=sl, psg=psg: e.matmul(
                            psg[:, w0:w1],
                            wgl[sl][:, dt * 128:(dt + 1) * 128],
                            bass.AP(oaA, XQC_OFF + dt * PSL + w0,
                                    [[NT * T_TOK, 128], [1, w1 - w0]]),
                            start=(dt == 0), stop=(dt == NT - 1)))
                        S("t", "T" if (dt == NT - 1 and n == nmm - 1) else None, 1,
                          lambda e, dt=dt, w0=w0, w1=w1, sl=sl, psu=psu: e.matmul(
                              psu[:, w0:w1],
                              wul[sl][:, dt * 128:(dt + 1) * 128],
                              bass.AP(oaA, XQC_OFF + dt * PSL + w0,
                                      [[NT * T_TOK, 128], [1, w1 - w0]]),
                              start=(dt == 0), stop=(dt == NT - 1)))
                wgs_use[sl] = C["T"]
                wus_use[sl] = C["T"]
                W("s", "T")
                need("s", "gsl%d" % sl)
                S("s", "S", 1, lambda e, tw=tw, psg=psg, gsl=gsl: e.activation(
                    gsl[:, 0:tw], psg[:, 0:tw], AF.Silu))
                done("p4g%d" % sl, "S")
                sil = C["S"]
                WT("v", "S", sil)
                S("v", "V", 1, lambda e, ht=ht, tw=tw, psu=psu, gsl=gsl: e.tensor_mul(
                    p_slab(ht)[:, 0:tw], psu[:, 0:tw], gsl[:, 0:tw]))
                done("ps2a" if sl == 0 else "ps2b", "V")
                done("gsl%d" % sl, "V")
            done("pfull", "V")
            gu_T = C["T"]

            tag(f"P4.dn{tc}")
            for dt in range(NT):
                psd = ps4a[:, (dt % 2) * 1024:(dt % 2) * 1024 + tw]
                if dt < 2:
                    need("t", "pfull", "p4g%d" % (dt % 2))
                need("t", "p4d%d" % (dt % 2))
                for ht in range(NHT):
                    hsl = (ht // 16) % 2
                    hidx = ht % 16
                    if hidx == 0:
                        if dt == 0 and ht == 0 and tc == 0:
                            W("g", "PP")
                        WT("g", "T", wds_use[hsl])
                        S("g", ["FD0", "FD1"][hsl], 16,
                          lambda e, dt=dt, ht=ht, hsl=hsl: e.dma_start(
                              r3(wdl[hsl][:], 128),
                              wdb[dt, ht:ht + 16, :, :].rearrange("a p b -> p a b")))
                        W("s", ["FD0", "FD1"][hsl])
                        WT("s", "T", wds_use[hsl])
                        sd = S("s", "S", 1, lambda e, hsl=hsl: e.sign(
                            wdl[hsl][:], wdl[hsl][:]))
                        wd_free[hsl] = sd
                        WT("t", "S", sd)
                    for n in range(nmm):
                        w0 = n * 512
                        w1 = min(tw, w0 + 512)
                        S("t", "T" if (hidx == 15 and n == nmm - 1) else None, 1,
                          lambda e, ht=ht, hidx=hidx, hsl=hsl, w0=w0, w1=w1, psd=psd: e.matmul(
                              psd[:, w0:w1],
                              wdl[hsl][:, hidx * 128:(hidx + 1) * 128],
                              p_slab(ht)[:, w0:w1],
                              start=(ht == 0), stop=(ht == NHT - 1)))
                    if hidx == 15:
                        wds_use[hsl] = C["T"]
                W("s", "T")
                S("s", "S", 1, lambda e, dt=dt, tw=tw, psd=psd: e.activation(
                    bass.AP(oaA, GF_OFF + dt * PSL, [[NT * T_TOK, 128], [1, tw]]),
                    psd, AF.Silu))
                done("p4d%d" % (dt % 2), "S")
                done("gf%d" % dt, "S")

            tag(f"P4.cmb{tc}")
            ntt = tw // 128
            for ttl in range(ntt):
                tt = (t0 // 128) + ttl
                WT("y", "V", cx_free)
                S("y", "CX0", 16, lambda e, tt=tt: e.dma_start(
                    f32a[:], x_d[tt * 128:(tt + 1) * 128, :]))
                WT("y", "V", cs_free)
                S("y", "CS0", 16, lambda e, tt=tt: e.dma_start(
                    oa(SAT_OFF, 2048), sat_sp[tt, :, :]))
                for grp in range(4):
                    pp = ps2a if grp % 2 == 0 else ps2b
                    ppres = "ps2a" if grp % 2 == 0 else "ps2b"
                    need("t", ppres)
                    for j in range(4):
                        dt = grp * 4 + j
                        need("t", "gf%d" % dt)
                        S("t", "T" if j == 3 else None, 1,
                          lambda e, dt=dt, j=j, pp=pp, ttl=ttl: e.matmul(
                              pp[:, j * 128:(j + 1) * 128],
                              bass.AP(oaA, GF_OFF + dt * PSL + ttl * 128,
                                      [[NT * T_TOK, 128], [1, 128]]),
                              cIb[:], start=True, stop=True))
                    W("v", "T")
                    S("v", "V", 1, lambda e, grp=grp, pp=pp: e.tensor_copy(
                        oa(GT_OFF + grp * 512, 512), pp[:, 0:512]))
                    done(ppres, "V")
                W("v", "CX0", "CS0")
                WT("v", "CO0", co_free)
                cs_free = S("v", "V", 1, lambda e: e.tensor_mul(
                    okn[:], oa(GT_OFF, 2048), oa(SAT_OFF, 2048)))
                cx_free = S("v", "V", 1, lambda e: e.tensor_add(
                    f32b[:], okn[:], f32a[:]))
                W("y", "V")
                co_free = S("y", "CO0", 16, lambda e, tt=tt: e.dma_start(
                    out_d[tt * 128:(tt + 1) * 128, :], f32b[:]))

        W("y", "CO0")
        if DEBUG:
            W("y", "YC")

    with es, nc.Block() as block:
        @block.tensor
        def _(t):
            prog("t", t)

        @block.vector
        def _(v):
            prog("v", v)

        @block.scalar
        def _(s):
            prog("s", s)

        @block.gpsimd
        def _(g):
            prog("g", g)

        @block.sync
        def _(y):
            prog("y", y)

    return nc


def host_pack(inputs):
    x = np.ascontiguousarray(np.asarray(inputs["L0_attn_hidden_states"], dtype=np.float32))
    Wq = np.asarray(inputs["L0_attn_Wq"])
    Wk = np.asarray(inputs["L0_attn_Wk"])
    Wv = np.asarray(inputs["L0_attn_Wv"])
    Wg = np.asarray(inputs["L0_attn_Wg"])
    Ws = np.asarray(inputs["L0_attn_Ws"])
    Wo = np.asarray(inputs["L0_attn_Wo"])
    gnw = np.ascontiguousarray(np.asarray(inputs["L0_attn_g_norm_weight"], dtype=np.float32).reshape(128, 1))
    Wgf = np.asarray(inputs["L0_ffn_W_g"])
    Wuf = np.asarray(inputs["L0_ffn_W_u"])
    Wdf = np.asarray(inputs["L0_ffn_W_d"])

    fts = []
    for p in range(8):
        h0, h1 = 2 * p, 2 * p + 1
        fts.append(np.concatenate([Wq[:, h0 * 64:(h0 + 1) * 64], Wq[:, h1 * 64:(h1 + 1) * 64]], 1))
        fts.append(np.concatenate([Wk[:, h0 * 64:(h0 + 1) * 64], Wk[:, h1 * 64:(h1 + 1) * 64]], 1))
        fts.append(np.concatenate([Ws[:, h0 * 64:(h0 + 1) * 64], Ws[:, h1 * 64:(h1 + 1) * 64]], 1))
        fts.append(Wv[:, h0 * 128:(h0 + 1) * 128])
        fts.append(Wv[:, h1 * 128:(h1 + 1) * 128])
        fts.append(Wg[:, h0 * 128:(h0 + 1) * 128])
        fts.append(Wg[:, h1 * 128:(h1 + 1) * 128])
    attn_w = np.ascontiguousarray(np.stack(fts).reshape(56, 16, 128, 128))
    wo_w = np.ascontiguousarray(Wo.reshape(16, 128, D))
    wg_s = np.ascontiguousarray(Wgf.reshape(64, 128, 16, 128).transpose(0, 2, 3, 1))
    wu_s = np.ascontiguousarray(Wuf.reshape(64, 128, 16, 128).transpose(0, 2, 3, 1))
    wd_s = np.ascontiguousarray(Wdf.reshape(16, 128, 64, 128).transpose(0, 2, 3, 1))
    ident_bf = np.eye(128, dtype=ml_dtypes.bfloat16)
    ident_f32 = np.eye(128, dtype=np.float32)
    triu_bf = np.triu(np.ones((128, 128), np.float32)).astype(ml_dtypes.bfloat16)

    shared = dict(attn_w=attn_w, wo_w=wo_w, wg_s=wg_s, wu_s=wu_s, wd_s=wd_s,
                  gnw=gnw, ident_bf=ident_bf, ident_f32=ident_f32,
                  triu_bf=triu_bf)
    return x, shared


def kernel(**inputs):
    x, shared = host_pack(inputs)
    nc = build_nc()
    in_maps = [dict(shared, x=np.ascontiguousarray(x[c])) for c in range(8)]
    res = run_bass_kernel_spmd(nc, in_maps, core_ids=list(range(8)))
    return np.stack([np.asarray(res.results[c]["out"]) for c in range(8)], axis=0)


# version 05
# speedup vs baseline: 1.1806x; 1.1457x over previous
import numpy as np
import ml_dtypes
import concourse.bass as bass
import concourse.mybir as mybir
from concourse.bass_utils import run_bass_kernel_spmd

BF = mybir.dt.bfloat16
F32 = mybir.dt.float32
I32 = mybir.dt.int32
U8 = mybir.dt.uint8
AF = mybir.ActivationFunctionType
ALU = mybir.AluOpType
AX = mybir.AxisListType

T_TOK = 2048
D = 2048
NT = 16
NP = 8
NCH = 16
NHT = 64
TCH = [768, 768, 512]
TOFF = [0, 768, 1536]
PSL = 768

DEBUG = False
TAGS = []


def build_nc():
    nc = bass.Bass("TRN2", target_bir_lowering=False,
                   detect_race_conditions=False)

    x_d = nc.dram_tensor("x", [T_TOK, D], F32, kind="ExternalInput")
    aw_d = nc.dram_tensor("attn_w", [56, 16, 128, 128], F32, kind="ExternalInput")
    wo_d = nc.dram_tensor("wo_w", [16, 128, D], F32, kind="ExternalInput")
    wg_d = nc.dram_tensor("wg_s", [64, 16, 128, 128], F32, kind="ExternalInput")
    wu_d = nc.dram_tensor("wu_s", [64, 16, 128, 128], F32, kind="ExternalInput")
    wd_d = nc.dram_tensor("wd_s", [16, 64, 128, 128], F32, kind="ExternalInput")
    gnw_d = nc.dram_tensor("gnw", [128, 1], F32, kind="ExternalInput")
    ib_d = nc.dram_tensor("ident_bf", [128, 128], BF, kind="ExternalInput")
    if_d = nc.dram_tensor("ident_f32", [128, 128], F32, kind="ExternalInput")
    tri_d = nc.dram_tensor("triu_bf", [128, 128], BF, kind="ExternalInput")
    out_d = nc.dram_tensor("out", [T_TOK, D], F32, kind="ExternalOutput")
    xq_sp = nc.dram_tensor("xq_sp", [NT, 128, T_TOK], BF, kind="Internal")
    wgb = nc.dram_tensor("wgb", [64, 16, 128, 128], BF, kind="Internal")
    wub = nc.dram_tensor("wub", [64, 16, 128, 128], BF, kind="Internal")
    wdb = nc.dram_tensor("wdb", [16, 64, 128, 128], BF, kind="Internal")
    sat_sp = nc.dram_tensor("sat_sp", [NT, 128, D], BF, kind="Internal")

    dbg = {}
    if DEBUG:
        dbg["xnT"] = nc.dram_tensor("dbg_xnT", [NT, 128, T_TOK], BF, kind="ExternalOutput")
        dbg["xq"] = nc.dram_tensor("dbg_xq", [NT, 128, T_TOK], BF, kind="ExternalOutput")
        dbg["oall"] = nc.dram_tensor("dbg_oall", [16, 128, T_TOK], BF, kind="ExternalOutput")
        dbg["sat"] = nc.dram_tensor("dbg_sat", [NT, 128, D], BF, kind="ExternalOutput")
        dbg["st"] = nc.dram_tensor("dbg_st", [128, 6 * NT], F32, kind="ExternalOutput")

    from contextlib import ExitStack
    es = ExitStack()

    def sbuf(name, shape, dt):
        return es.enter_context(nc.sbuf_tensor(name, shape, dt))

    xnA = sbuf("xnA", [128, NT * T_TOK], BF)
    oaA = sbuf("oaA", [128, NT * T_TOK], BF)
    GF_OFF = 0
    XQC_OFF = 12288
    GT_OFF = 24576
    SAT_OFF = 26624
    POV_OFF = 28672

    cIb = sbuf("cIb", [128, 128], BF)
    cIf = sbuf("cIf", [128, 128], F32)
    cTri = sbuf("cTri", [128, 128], BF)
    cGnw = sbuf("cGnw", [128, 1], F32)
    cZ = sbuf("cZ", [128, 1], F32)

    stats = {}
    for nm in ("s1", "s2", "m", "r", "nb", "s2q", "amax", "r2", "sq", "inv",
               "sq2r", "t1"):
        stats[nm] = sbuf("st_" + nm, [128, NT], F32)
    smx = sbuf("smx", [128, 2 * NCH], F32)
    ssm = sbuf("ssm", [128, 2 * NCH], F32)
    srs = sbuf("srs", [128, 2 * NCH], F32)
    sos = sbuf("sos", [128, 2 * NCH], F32)
    sor = sbuf("sor", [128, 2 * NCH], F32)

    f32a = sbuf("f32a", [128, 2048], F32)
    f32b = sbuf("f32b", [128, 2048], F32)
    okn = sbuf("okn", [128, 2048], F32)

    AR = 54272
    arena = sbuf("arena", [128, AR], U8)
    ar_base = nc.lookup_mloc(arena).addr

    def at(name, off, shape, dt):
        return nc.alloc_sbuf_tensor_at(name, shape, dt, offset=ar_base + off)

    qqT = at("qqT", 0, [128, 2048], BF)
    kkpw = at("kkpw", 4096, [128, 2048], BF)
    embf = at("embf", 8192, [128, 2048], BF)
    etok = at("etok", 12288, [128, 2048], BF)
    ktok = at("ktok", 16384, [128, 2048], BF)
    vt0 = at("vt0", 20480, [128, 2048], BF)
    vt1 = at("vt1", 24576, [128, 2048], BF)
    gs0 = at("gs0", 28672, [128, 2048], BF)
    gs1 = at("gs1", 32768, [128, 2048], BF)
    sm0 = at("sm0", 36864, [128, 2048], BF)
    sm1 = at("sm1", 40960, [128, 2048], BF)
    pwT = at("pwT", 45056, [128, 2048], BF)
    scr4 = at("scr4", 49152, [128, 1024], F32)
    xqi = at("xqi", 20480, [128, 2048], I32)
    pext = at("pext", 0, [128, 17 * PSL], BF)
    wgl = [at("wgl0", 26624, [128, 2048], BF), at("wgl1", 30720, [128, 2048], BF)]
    wul = [at("wul0", 34816, [128, 2048], BF), at("wul1", 38912, [128, 2048], BF)]
    wdl = [at("wdl0", 43008, [128, 2048], BF), at("wdl1", 47104, [128, 2048], BF)]
    gsil = at("gsil", 51200, [128, 768], BF)
    gsilB = at("gsilB", 52736, [128, 768], BF)

    ps4a = es.enter_context(nc.psum_tensor("ps4a", [128, 2048], F32))
    ps2a = es.enter_context(nc.psum_tensor("ps2a", [128, 1024], F32))
    ps2b = es.enter_context(nc.psum_tensor("ps2b", [128, 1024], F32))

    sem_names = ["T", "V", "S", "YC",
                 "XA", "XB", "XQ0", "XQ1",
                 "WA0", "WA1", "GWO", "SP0", "SP1",
                 "Q4", "WG0", "WG1", "WU0", "WU1", "FD0", "FD1", "PP",
                 "CX0", "CS0", "CO0"]
    sems = {n: nc.alloc_semaphore(name="sem_" + n) for n in sem_names}

    def zbc(n):
        return bass.AP(cZ, 0, [[1, 128], [0, n]])

    def tri_bc(nch):
        return bass.AP(cTri, 0, [[128, 128], [0, nch], [1, 128]])

    def r3(ap, b):
        return ap.rearrange("p (a b) -> p a b", b=b)

    def p_slab(ht):
        if ht < 42:
            return bass.AP(xnA, ht * PSL, [[NT * T_TOK, 128], [1, PSL]])
        if ht < 47:
            return bass.AP(oaA, POV_OFF + (ht - 42) * PSL, [[NT * T_TOK, 128], [1, PSL]])
        return bass.AP(pext, (ht - 47) * PSL, [[17 * PSL, 128], [1, PSL]])

    def oa(off, n):
        return bass.AP(oaA, off, [[NT * T_TOK, 128], [1, n]])

    def prog(me, eng):
        C = {k: 0 for k in sems}
        LW = {k: 0 for k in sems}
        _tmm = [0]
        rec = {}

        def tag(name):
            if me == "t":
                TAGS.append((name, _tmm[0]))

        def done(res, engsem):
            rec[res] = (engsem, C[engsem])

        def need(who, *ress):
            for res in ress:
                e_, c_ = rec.get(res, (None, 0))
                if e_ is not None:
                    WT(who, e_, c_)

        def W(who, *names):
            if who != me:
                return
            for n_ in names:
                if C[n_] > 0 and LW[n_] < C[n_]:
                    eng.wait_ge(sems[n_], C[n_])
                    LW[n_] = C[n_]

        def WT(who, sem, val):
            if who == me and val > 0 and LW[sem] < val:
                eng.wait_ge(sems[sem], val)
                LW[sem] = val

        def S(owner, sem, n, f):
            inst = f(eng) if owner == me else None
            if owner == "t":
                _tmm[0] += 1
            if sem is not None:
                C[sem] += n
                if inst is not None:
                    inst.then_inc(sems[sem], n)
            return C[sem] if sem is not None else None

        S("y", "YC", 16, lambda e: e.dma_start(cIb[:], ib_d[:, :]))
        S("y", "YC", 16, lambda e: e.dma_start(cIf[:], if_d[:, :]))
        S("y", "YC", 16, lambda e: e.dma_start(cTri[:], tri_d[:, :]))
        S("y", "YC", 16, lambda e: e.dma_start(cGnw[:], gnw_d[:, :]))
        W("v", "YC")
        S("v", "V", 1, lambda e: e.memset(cZ[:], 0.0))

        for k in range(8):
            S("g", "PP", 16, lambda e, k=k: e.dma_start(
                wgb[k * 8:(k + 1) * 8, :, :, :], wg_d[k * 8:(k + 1) * 8, :, :, :]))
            S("g", "PP", 16, lambda e, k=k: e.dma_start(
                wub[k * 8:(k + 1) * 8, :, :, :], wu_d[k * 8:(k + 1) * 8, :, :, :]))
            S("g", "PP", 16, lambda e, k=k: e.dma_start(
                wdb[k * 2:(k + 1) * 2, :, :, :], wd_d[k * 2:(k + 1) * 2, :, :, :]))

        tag("P1")
        xslot = [f32a, f32b]
        xsem = ["XA", "XB"]
        xfree = [0, 0]
        xfree_s = [0, 0]
        qstg = [etok, ktok]
        qsem = ["XQ0", "XQ1"]
        qfree = [0, 0]

        def stv(name, f):
            W("v", "V", "S")
            S("v", "V", 1, f)

        for i in range(NT):
            sl = i % 2
            xt = xslot[sl]
            WT("y", "V", xfree[sl])
            WT("y", "S", xfree_s[sl])
            S("y", xsem[sl], 16, lambda e, i=i, xt=xt: e.dma_start(
                xt[:], x_d[i * 128:(i + 1) * 128, :]))
            W("s", xsem[sl])
            xfree_s[sl] = S("s", "S", 1, lambda e, i=i, xt=xt: e.activation(
                oa(15 * T_TOK, 2048), xt[:], AF.Square,
                accum_out=stats["s2"][:, i:i + 1]))
            W("v", xsem[sl])
            xfree[sl] = S("v", "V", 1, lambda e, i=i, xt=xt: e.reduce_sum(
                stats["s1"][:, i:i + 1], xt[:], axis=AX.X))
        W("v", "S")
        stv("m", lambda e: e.tensor_scalar_mul(stats["m"][:, :], stats["s1"][:, :], 1.0 / D))
        stv("t1", lambda e: e.tensor_mul(stats["t1"][:, :], stats["m"][:, :], stats["m"][:, :]))
        stv("v1", lambda e: e.tensor_scalar_mul(stats["r"][:, :], stats["s2"][:, :], 1.0 / D))
        stv("v2", lambda e: e.tensor_sub(stats["r"][:, :], stats["r"][:, :], stats["t1"][:, :]))
        stv("v3", lambda e: e.tensor_scalar_add(stats["r"][:, :], stats["r"][:, :], 1e-5))
        W("s", "V")
        S("s", "S", 1, lambda e: e.activation(stats["r"][:, :], stats["r"][:, :], AF.Sqrt))
        W("v", "S")
        stv("v4", lambda e: e.reciprocal(stats["r"][:, :], stats["r"][:, :]))
        stv("v5", lambda e: e.tensor_mul(stats["nb"][:, :], stats["m"][:, :], stats["r"][:, :]))
        stv("v6", lambda e: e.tensor_scalar_mul(stats["nb"][:, :], stats["nb"][:, :], -1.0))
        done("LN", "V")

        for i in range(NT):
            sl = i % 2
            xt = xslot[sl]
            WT("y", "V", xfree[sl])
            WT("y", "S", xfree_s[sl])
            S("y", xsem[sl], 16, lambda e, i=i, xt=xt: e.dma_start(
                xt[:], x_d[i * 128:(i + 1) * 128, :]))
            W("v", xsem[sl])
            need("v", "LN")
            W("v", "S")
            S("v", "V", 1, lambda e, i=i, xt=xt: e.scalar_tensor_tensor(
                okn[:], xt[:], stats["r"][:, i:i + 1],
                bass.AP(stats["nb"], i, [[NT, 128], [0, 2048]]),
                ALU.mult, ALU.add))
            xfree[sl] = C["V"]
            xfree_s[sl] = 0
            W("s", "V")
            S("s", "S", 1, lambda e, i=i: e.activation(
                oa(15 * T_TOK, 2048), okn[:], AF.Square,
                accum_out=stats["s2q"][:, i:i + 1]))
            S("v", "V", 1, lambda e, i=i: e.reduce_max(
                stats["amax"][:, i:i + 1], okn[:], axis=AX.X,
                apply_absolute_value=True))
            W("v", "T")
            S("v", "V", 1, lambda e: e.tensor_copy(sm0[:], okn[:]))
            W("t", "YC")
            for grp in range(4):
                pp = ps2a if grp % 2 == 0 else ps2b
                W("t", "V", "S")
                for j in range(4):
                    dt = grp * 4 + j
                    S("t", "T" if j == 3 else None, 1, lambda e, dt=dt, j=j, pp=pp: e.matmul(
                        pp[:, j * 128:(j + 1) * 128],
                        sm0[:, dt * 128:(dt + 1) * 128], cIb[:],
                        start=True, stop=True))
                W("s", "T")
                S("s", "S", 1, lambda e, grp=grp, pp=pp, i=i: e.copy(
                    bass.AP(xnA, (grp * 4) * T_TOK + i * 128,
                            [[NT * T_TOK, 128], [T_TOK, 4], [1, 128]]),
                    pp[:, 0:512]))
            done("xnT%d" % i, "S")
        W("v", "S")
        stv("q1", lambda e: e.tensor_scalar(stats["r2"][:, :], stats["s2q"][:, :],
                                            1.0 / D, 1e-8, ALU.mult, ALU.add))
        W("s", "V")
        S("s", "S", 1, lambda e: e.activation(stats["r2"][:, :], stats["r2"][:, :], AF.Sqrt))
        W("v", "S")
        stv("q2", lambda e: e.reciprocal(stats["r2"][:, :], stats["r2"][:, :]))
        stv("q3", lambda e: e.tensor_mul(stats["amax"][:, :], stats["amax"][:, :], stats["r2"][:, :]))
        stv("q4", lambda e: e.reciprocal(stats["sq"][:, :], stats["amax"][:, :]))
        stv("q5", lambda e: e.tensor_scalar_mul(stats["sq"][:, :], stats["sq"][:, :], 127.0))
        stv("q6", lambda e: e.reciprocal(stats["inv"][:, :], stats["sq"][:, :]))
        stv("q7", lambda e: e.tensor_mul(stats["sq2r"][:, :], stats["sq"][:, :], stats["r2"][:, :]))
        stv("q8", lambda e: e.tensor_mul(stats["t1"][:, :], stats["r"][:, :], stats["sq2r"][:, :]))
        stv("q9", lambda e: e.tensor_mul(stats["s1"][:, :], stats["nb"][:, :], stats["sq2r"][:, :]))
        done("QS", "V")

        for i in range(NT):
            sl = i % 2
            xt = xslot[sl]
            WT("y", "V", xfree[sl])
            S("y", xsem[sl], 16, lambda e, i=i, xt=xt: e.dma_start(
                xt[:], x_d[i * 128:(i + 1) * 128, :]))
            W("v", xsem[sl])
            need("v", "QS")
            S("v", "V", 1, lambda e, i=i, xt=xt: e.scalar_tensor_tensor(
                okn[:], xt[:], stats["t1"][:, i:i + 1],
                bass.AP(stats["s1"], i, [[NT, 128], [0, 2048]]),
                ALU.mult, ALU.add))
            xfree[sl] = C["V"]
            S("v", "V", 1, lambda e: e.tensor_copy(xqi[:], okn[:]))
            W("v", "V", "T")
            S("v", "V", 1, lambda e, i=i: e.tensor_scalar_mul(
                sm1[:], xqi[:], stats["inv"][:, i:i + 1]))
            qsl = i % 2
            stg = qstg[qsl]
            for grp in range(4):
                pp = ps2a if grp % 2 == 0 else ps2b
                W("t", "V", "S")
                WT("t", qsem[qsl], qfree[qsl])
                for j in range(4):
                    dt = grp * 4 + j
                    S("t", "T" if j == 3 else None, 1, lambda e, dt=dt, j=j, pp=pp: e.matmul(
                        pp[:, j * 128:(j + 1) * 128],
                        sm1[:, dt * 128:(dt + 1) * 128], cIb[:],
                        start=True, stop=True))
                W("s", "T")
                S("s", "S", 1, lambda e, grp=grp, pp=pp, stg=stg: e.copy(
                    stg[:, grp * 512:(grp + 1) * 512], pp[:, 0:512]))
            W("y", "S")
            qfree[qsl] = S("y", qsem[qsl], 16, lambda e, i=i, stg=stg: e.dma_start(
                xq_sp[:, :, i * 128:(i + 1) * 128].rearrange("a p b -> p a b"),
                r3(stg[:], 128)))

        if DEBUG:
            W("y", "S")
            S("y", "YC", 16, lambda e: e.dma_start(
                dbg["xnT"][:, :, :].rearrange("a p b -> p a b"),
                r3(xnA[:], T_TOK)))
            W("y", "XQ0", "XQ1")
            S("y", "YC", 16, lambda e: e.dma_start(
                dbg["xq"][:, :, :], xq_sp[:, :, :]))

        tag("P2")
        wfree = [0, 0]

        def wp_ap(p, par, tile=None):
            off = (2 * p + par) * T_TOK
            if tile is None:
                return bass.AP(oaA, off, [[NT * T_TOK, 128], [1, 2048]])
            return bass.AP(oaA, off + tile * 128, [[NT * T_TOK, 128], [1, 128]])

        def tgroup(src, dst, fp32, src_res=None, dst_res=None, who_evac="s"):
            esem = "V" if who_evac == "v" else "S"
            for grp in range(4):
                pp = ps2a if grp % 2 == 0 else ps2b
                ppres = "ps2a" if grp % 2 == 0 else "ps2b"
                if src_res:
                    need("t", src_res)
                need("t", ppres)
                for j in range(4):
                    c = grp * 4 + j
                    if fp32:
                        S("t", "T" if j == 3 else None, 1,
                          lambda e, c=c, j=j, pp=pp, src=src: e.transpose(
                              pp[:, j * 128:(j + 1) * 128],
                              src[:, c * 128:(c + 1) * 128], cIf[:]))
                    else:
                        S("t", "T" if j == 3 else None, 1,
                          lambda e, c=c, j=j, pp=pp, src=src: e.matmul(
                              pp[:, j * 128:(j + 1) * 128],
                              src[:, c * 128:(c + 1) * 128], cIb[:],
                              start=True, stop=True))
                W(who_evac, "T")
                if who_evac == "s":
                    S("s", "S", 1, lambda e, grp=grp, pp=pp, dst=dst: e.copy(
                        dst[:, grp * 512:(grp + 1) * 512], pp[:, 0:512]))
                else:
                    S("v", "V", 1, lambda e, grp=grp, pp=pp, dst=dst: e.tensor_copy(
                        dst[:, grp * 512:(grp + 1) * 512], pp[:, 0:512]))
                done(ppres, esem)
            if dst_res:
                done(dst_res, esem)

        def wp_load(p, f):
            par = f % 2
            WT("g", "T", wfree[par])
            S("g", ["WA0", "WA1"][par], 16, lambda e, p=p, f=f, par=par: e.dma_start(
                r3(wp_ap(p, par), 128),
                aw_d[p * 7 + f, :, :, :].rearrange("a p b -> p a b")))

        for p in range(NP):
            tag(f"P2.proj")
            wfree[0] = wfree[1] = 0
            wp_load(p, 0)
            wp_load(p, 1)
            for f in range(7):
                par = f % 2
                W("t", ["WA0", "WA1"][par])
                for n in range(4):
                    if p == 0:
                        need("t", "xnT%d" % (4 * n + 3))
                    need("t", "psA%d" % n)
                    for dt in range(NT):
                        S("t", "T" if dt == NT - 1 else None, 1,
                          lambda e, n=n, dt=dt, p=p, par=par: e.matmul(
                              ps4a[:, n * 512:(n + 1) * 512],
                              wp_ap(p, par, dt),
                              bass.AP(xnA, dt * T_TOK + n * 512, [[NT * T_TOK, 128], [1, 512]]),
                              start=(dt == 0), stop=(dt == NT - 1)))
                    if n == 3:
                        wfree[par] = C["T"]
                        if f + 2 < 7:
                            wp_load(p, f + 2)
                    pc = ps4a[:, n * 512:(n + 1) * 512]
                    if f == 2:
                        W("v", "T")
                        S("v", "V", 1, lambda e, n=n, pc=pc: e.tensor_scalar(
                            f32a[:, n * 512:(n + 1) * 512], pc,
                            -32.0, 32.0, ALU.max, ALU.min))
                        done("psA%d" % n, "V")
                    else:
                        W("s", "T")
                        if f == 0:
                            S("s", "S", 1, lambda e, n=n, pc=pc: e.mul(
                                qqT[:, n * 512:(n + 1) * 512], pc, 0.125))
                        elif f == 1:
                            S("s", "S", 1, lambda e, n=n, pc=pc: e.copy(
                                kkpw[:, n * 512:(n + 1) * 512], pc))
                        elif f in (3, 4):
                            vtb = vt0 if f == 3 else vt1
                            S("s", "S", 1, lambda e, n=n, pc=pc, vtb=vtb: e.copy(
                                vtb[:, n * 512:(n + 1) * 512], pc))
                        else:
                            gsb = gs0 if f == 5 else gs1
                            S("s", "S", 1, lambda e, n=n, pc=pc, gsb=gsb: e.activation(
                                gsb[:, n * 512:(n + 1) * 512], pc, AF.Silu))
                        done("psA%d" % n, "S")
                if f == 0:
                    done("qqT", "S")
                elif f == 1:
                    done("kkT", "S")
                elif f == 2:
                    done("sclip", "V")
                elif f == 3:
                    done("vt0", "S")
                elif f == 4:
                    done("vt1", "S")
                elif f == 5:
                    done("gs0", "S")
                elif f == 6:
                    done("gs1", "S")
                if f == 2:
                    need("s", "sclip")
                    S("s", "S", 1, lambda e: e.activation(f32b[:], f32a[:], AF.Exp))
                    done("E32", "S")
                    need("v", "E32")
                    S("v", "V", 1, lambda e: e.tensor_tensor_scan(
                        f32a[:], f32b[:], zbc(2048), 0.0, ALU.add, ALU.add))
                    S("v", "V", 1, lambda e: e.tensor_copy(embf[:], f32b[:]))
                    done("embf", "V")
                    S("v", "V", 1, lambda e: e.reciprocal(f32b[:], f32a[:]))
                    done("recipAm", "V")

            tag("P2.tr")
            if p == 0:
                W("s", "XQ0", "XQ1")
            tgroup(embf, etok, False, "embf", "etok")
            tgroup(kkpw, ktok, False, "kkT", "Ktok")
            for vtb, stg, rr in ((vt0, sm0, "vt0"), (vt1, sm1, "vt1")):
                tgroup(vtb, stg, False, rr, rr + "st")
                need("v", rr + "st")
                S("v", "V", 1, lambda e, vtb=vtb, stg=stg: e.tensor_copy(vtb[:], stg[:]))
                done(rr + "tok", "V")
            tgroup(f32b, f32a, True, "recipAm", "recipA")

            tag("P2.pk")
            for rnd in range(2):
                need("t", "Ktok", "etok", "ps2b")
                if rnd == 1:
                    W("t", "V")
                for cc in range(8):
                    c = rnd * 8 + cc
                    S("t", "T" if cc == 7 else None, 1, lambda e, c=c, cc=cc: e.matmul(
                        ps2b[:, cc * 128:(cc + 1) * 128],
                        ktok[:, c * 128:(c + 1) * 128],
                        etok[:, c * 128:(c + 1) * 128],
                        start=True, stop=True))
                W("v", "T")
                if rnd == 0:
                    S("v", "V", 1, lambda e: e.memset(f32b[:, 0:128], 0.0))
                for cc in range(8):
                    c = rnd * 8 + cc
                    if c == NCH - 1:
                        break
                    S("v", "V", 1, lambda e, c=c, cc=cc: e.tensor_add(
                        f32b[:, (c + 1) * 128:(c + 2) * 128],
                        f32b[:, c * 128:(c + 1) * 128],
                        ps2b[:, cc * 128:(cc + 1) * 128]))
            S("v", "V", 1, lambda e: e.tensor_copy(ktok[:], f32b[:]))
            done("Hk", "V")
            done("ps2b", "V")

            tag("P2.sok")
            for h in range(2):
                smh = sm0 if h == 0 else sm1
                pso = ps2a if h == 0 else ps2b
                okh = okn[:, h * 1024:(h + 1) * 1024]
                for c in range(NCH):
                    if c % 4 == 0:
                        need("t", "kkT", "qqT", "psA%d" % (c // 4))
                    S("t", "T" if c % 4 == 3 else None, 1, lambda e, c=c, h=h: e.matmul(
                        ps4a[:, c * 128:(c + 1) * 128],
                        kkpw[h * 64:(h + 1) * 64, c * 128:(c + 1) * 128],
                        qqT[h * 64:(h + 1) * 64, c * 128:(c + 1) * 128],
                        start=True, stop=True))
                    if c % 4 == 3:
                        W("v", "T")
                        g4 = c // 4
                        S("v", "V", 1, lambda e, g4=g4, smh=smh: e.tensor_tensor(
                            r3(smh[:, g4 * 512:(g4 + 1) * 512], 128),
                            r3(ps4a[:, g4 * 512:(g4 + 1) * 512], 128),
                            tri_bc(4), ALU.mult))
                        done("psA%d" % g4, "V")
                        done("smm%d_%d" % (h, g4), "V")
                need("t", "etok", "Hk", "ps2a" if h == 0 else "ps2b")
                for c in range(NCH):
                    last = (c == NCH - 1)
                    if c % 4 == 0:
                        need("t", "smm%d_%d" % (h, c // 4))
                    S("t", None, 1, lambda e, c=c, h=h, smh=smh, pso=pso: e.matmul(
                        pso[:, c * 64:(c + 1) * 64],
                        smh[:, c * 128:(c + 1) * 128],
                        etok[:, c * 128 + h * 64: c * 128 + h * 64 + 64],
                        start=True, stop=(c == 0)))
                    if c > 0:
                        S("t", "T" if last else None, 1, lambda e, c=c, h=h: e.matmul(
                            pso[:, c * 64:(c + 1) * 64],
                            qqT[h * 64:(h + 1) * 64, c * 128:(c + 1) * 128],
                            ktok[h * 64:(h + 1) * 64, c * 128 + h * 64: c * 128 + h * 64 + 64],
                            start=False, stop=True))
                W("v", "T")
                need("v", "recipA")
                S("v", "V", 1, lambda e, okh=okh, h=h, pso=pso: e.tensor_tensor(
                    r3(okh, 64), r3(pso[:], 64),
                    bass.AP(f32a, h * 64, [[2048, 128], [128, NCH], [1, 64]]),
                    ALU.mult))
                done("ps2a" if h == 0 else "ps2b", "V")
                S("v", "V", 1, lambda e, okh=okh, h=h: e.reduce_max(
                    smx[:, h * NCH:(h + 1) * NCH], r3(okh, 64),
                    axis=AX.X, negate=True))
                done("smx%d" % h, "V")
            for h in range(2):
                okh = okn[:, h * 1024:(h + 1) * 1024]
                need("s", "smx%d" % h)
                if h == 1:
                    W("s", "V")
                for c in range(NCH):
                    S("s", "S" if c == NCH - 1 else None, 1, lambda e, c=c, okh=okh, h=h: e.activation(
                        scr4[:, c * 64:(c + 1) * 64], okh[:, c * 64:(c + 1) * 64],
                        AF.Exp, bias=smx[:, h * NCH + c:h * NCH + c + 1],
                        accum_out=ssm[:, h * NCH + c:h * NCH + c + 1]))
                done("exp%d" % h, "S")
                need("v", "exp%d" % h)
                W("v", "T")
                W("v", "V")
                S("v", "V", 1, lambda e, h=h: e.reciprocal(
                    srs[:, h * NCH:(h + 1) * NCH], ssm[:, h * NCH:(h + 1) * NCH]))
                W("v", "V")
                S("v", "V", 1, lambda e, okh=okh, h=h: e.tensor_tensor(
                    r3(okh, 64),
                    bass.AP(f32a, h * 64, [[2048, 128], [128, NCH], [1, 64]]),
                    bass.AP(srs, h * NCH, [[2 * NCH, 128], [1, NCH], [0, 64]]),
                    ALU.mult))
                S("v", "V", 1, lambda e, okh=okh, h=h: e.tensor_tensor(
                    bass.AP(kkpw, h * 64, [[2048, 128], [128, NCH], [1, 64]]),
                    r3(scr4[:], 64), r3(okh, 64), ALU.mult))

            done("pw", "V")
            tag("P2.pw")
            tgroup(kkpw, pwT, False, "pw", "pwT")

            for rnd in range(2):
                need("t", "etok", "vt0tok", "vt1tok", "ps2b")
                W("t", "V")
                for cc in range(8):
                    c = rnd * 8 + cc
                    S("t", None, 1, lambda e, c=c, cc=cc: e.matmul(
                        ps2b[0:64, cc * 128:(cc + 1) * 128],
                        etok[:, c * 128: c * 128 + 64],
                        vt0[:, c * 128:(c + 1) * 128],
                        start=True, stop=True))
                    S("t", "T" if cc == 7 else None, 1, lambda e, c=c, cc=cc: e.matmul(
                        ps2b[64:128, cc * 128:(cc + 1) * 128],
                        etok[:, c * 128 + 64: c * 128 + 128],
                        vt1[:, c * 128:(c + 1) * 128],
                        start=True, stop=True))
                W("v", "T")
                if rnd == 0:
                    S("v", "V", 1, lambda e: e.memset(f32b[:, 0:128], 0.0))
                for cc in range(8):
                    c = rnd * 8 + cc
                    if c == NCH - 1:
                        break
                    S("v", "V", 1, lambda e, c=c, cc=cc: e.tensor_add(
                        f32b[:, (c + 1) * 128:(c + 2) * 128],
                        f32b[:, c * 128:(c + 1) * 128],
                        ps2b[:, cc * 128:(cc + 1) * 128]))
            S("v", "V", 1, lambda e: e.tensor_copy(qqT[:], f32b[:]))
            done("Hv", "V")
            done("ps2b", "V")

            tag("P2.ov")
            for h in range(2):
                smh = sm0 if h == 0 else sm1
                vth = vt0 if h == 0 else vt1
                gsh = gs0 if h == 0 else gs1
                hh = 2 * p + h
                for c in range(NCH):
                    if c % 4 == 0:
                        need("t", "embf", "pwT", "psA%d" % (c // 4))
                    S("t", "T" if c % 4 == 3 else None, 1, lambda e, c=c, h=h: e.matmul(
                        ps4a[:, c * 128:(c + 1) * 128],
                        embf[h * 64:(h + 1) * 64, c * 128:(c + 1) * 128],
                        pwT[h * 64:(h + 1) * 64, c * 128:(c + 1) * 128],
                        start=True, stop=True))
                    if c % 4 == 3:
                        W("v", "T")
                        g4 = c // 4
                        S("v", "V", 1, lambda e, g4=g4, smh=smh: e.tensor_tensor(
                            r3(smh[:, g4 * 512:(g4 + 1) * 512], 128),
                            r3(ps4a[:, g4 * 512:(g4 + 1) * 512], 128),
                            tri_bc(4), ALU.mult))
                        done("psA%d" % g4, "V")
                        done("smm%d_%d" % (h, g4), "V")
                need("t", "Hv", "vt%dtok" % h)
                for c in range(NCH):
                    last = (c == NCH - 1)
                    if c % 4 == 0:
                        need("t", "smm%d_%d" % (h, c // 4), "psA%d" % (c // 4))
                    S("t", None, 1, lambda e, c=c, smh=smh, vth=vth: e.matmul(
                        ps4a[:, c * 128:(c + 1) * 128],
                        smh[:, c * 128:(c + 1) * 128],
                        vth[:, c * 128:(c + 1) * 128],
                        start=True, stop=(c == 0)))
                    if c > 0:
                        S("t", "T" if last else None, 1, lambda e, c=c, h=h: e.matmul(
                            ps4a[:, c * 128:(c + 1) * 128],
                            pwT[h * 64:(h + 1) * 64, c * 128:(c + 1) * 128],
                            qqT[h * 64:(h + 1) * 64, c * 128:(c + 1) * 128],
                            start=False, stop=True))
                W("s", "T")
                need("s", "Hv")
                S("s", "S", 1, lambda e: e.activation(f32b[:], ps4a[:], AF.Square))
                W("v", "S")
                S("v", "V", 1, lambda e, h=h: e.reduce_sum(
                    sos[:, h * NCH:(h + 1) * NCH], r3(f32b[:], 128), axis=AX.X))
                W("v", "V")
                S("v", "V", 1, lambda e, h=h: e.tensor_scalar(
                    sor[:, h * NCH:(h + 1) * NCH], sos[:, h * NCH:(h + 1) * NCH],
                    1.0 / 128.0, 1e-5, ALU.mult, ALU.add))
                W("s", "V")
                S("s", "S", 1, lambda e, h=h: e.activation(
                    sor[:, h * NCH:(h + 1) * NCH], sor[:, h * NCH:(h + 1) * NCH],
                    AF.Sqrt))
                W("v", "S")
                W("v", "V")
                S("v", "V", 1, lambda e, h=h: e.reciprocal(
                    sor[:, h * NCH:(h + 1) * NCH], sor[:, h * NCH:(h + 1) * NCH]))
                W("v", "V")
                S("v", "V", 1, lambda e, h=h, smh=smh: e.tensor_tensor(
                    r3(smh[:], 128), r3(ps4a[:], 128),
                    bass.AP(sor, h * NCH, [[2 * NCH, 128], [1, NCH], [0, 128]]),
                    ALU.mult))
                for grp in range(4):
                    pp = ps2a if grp % 2 == 0 else ps2b
                    W("t", "V", "S")
                    for j in range(4):
                        c = grp * 4 + j
                        S("t", "T" if j == 3 else None, 1,
                          lambda e, c=c, j=j, pp=pp, smh=smh: e.matmul(
                              pp[:, j * 128:(j + 1) * 128],
                              smh[:, c * 128:(c + 1) * 128], cIb[:],
                              start=True, stop=True))
                    W("v", "T")
                    S("v", "V", 1, lambda e, grp=grp, pp=pp, gsh=gsh, hh=hh: e.scalar_tensor_tensor(
                        oa(hh * T_TOK + grp * 512, 512),
                        pp[:, 0:512], cGnw[:, 0:1],
                        gsh[:, grp * 512:(grp + 1) * 512],
                        ALU.mult, ALU.mult))

        p2_T = C["T"]

        if DEBUG:
            W("y", "V")
            S("y", "YC", 16, lambda e: e.dma_start(
                dbg["oall"][:, :, :].rearrange("a p b -> p a b"),
                r3(oaA[:], T_TOK)))

        tag("P3")
        WT("g", "T", p2_T)
        for vt in range(16):
            S("g", "GWO", 16, lambda e, vt=vt: e.dma_start(
                bass.AP(xnA, vt * T_TOK, [[NT * T_TOK, 128], [1, D]]),
                wo_d[vt, :, :]))
        sslot = [qqT, kkpw]
        ssemn = ["SP0", "SP1"]
        sfree = [0, 0]
        for tt in range(NT):
            sl = tt % 2
            stg = sslot[sl]
            W("t", "GWO")
            W("t", "V")
            for vt in range(16):
                for n in range(4):
                    if sl == 0:
                        tgt = ps4a[:, n * 512:(n + 1) * 512]
                    else:
                        tgt = (ps2a if n < 2 else ps2b)[:, (n % 2) * 512:((n % 2) + 1) * 512]
                    S("t", "T" if (vt == 15 and n == 3) else None, 1,
                      lambda e, vt=vt, n=n, tt=tt, tgt=tgt: e.matmul(
                          tgt,
                          oa(vt * T_TOK + tt * 128, 128),
                          bass.AP(xnA, vt * T_TOK + n * 512, [[NT * T_TOK, 128], [1, 512]]),
                          start=(vt == 0), stop=(vt == 15)))
            WT("v", ssemn[sl], sfree[sl])
            W("v", "T")
            if sl == 0:
                S("v", "V", 1, lambda e, stg=stg: e.tensor_copy(stg[:], ps4a[:]))
            else:
                S("v", "V", 1, lambda e, stg=stg: e.tensor_copy(stg[:, 0:1024], ps2a[:]))
                S("v", "V", 1, lambda e, stg=stg: e.tensor_copy(stg[:, 1024:2048], ps2b[:]))
            W("y", "V")
            sfree[sl] = S("y", ssemn[sl], 16, lambda e, tt=tt, stg=stg: e.dma_start(
                sat_sp[tt, :, :], stg[:]))

        p3_T = C["T"]

        if DEBUG:
            W("y", "SP0", "SP1")
            S("y", "YC", 16, lambda e: e.dma_start(
                dbg["sat"][:, :, :], sat_sp[:, :, :]))

        tag("P4")
        wg_free = [0, 0]
        wu_free = [0, 0]
        wd_free = [0, 0]
        wgs_use = [0, 0]
        wus_use = [0, 0]
        wds_use = [0, 0]
        gu_T = 0
        cx_free = 0
        cs_free = 0
        co_free = 0
        W("y", "XQ0", "XQ1", "SP0", "SP1")
        W("g", "V")
        W("t", "SP0", "SP1")

        for tc in range(3):
            tw = TCH[tc]
            t0 = TOFF[tc]
            nmm = (tw + 511) // 512
            WT("y", "T", gu_T if tc > 0 else p3_T)
            S("y", "Q4", 16, lambda e, t0=t0, tw=tw: e.dma_start(
                bass.AP(oaA, XQC_OFF, [[NT * T_TOK, 128], [PSL, 16], [1, tw]]),
                xq_sp[:, :, t0:t0 + tw].rearrange("a p b -> p a b")))

            tag(f"P4.gu{tc}")
            su_rec = {}

            def gu_load_sign(ht, tc=tc):
                sl = ht % 2
                if ht == 0 and tc == 0:
                    W("y", "PP")
                WT("y", "T", wgs_use[sl])
                S("y", ["WG0", "WG1"][sl], 16, lambda e, ht=ht, sl=sl: e.dma_start(
                    r3(wgl[sl][:], 128),
                    wgb[ht, :, :, :].rearrange("a p b -> p a b")))
                WT("y", "T", wus_use[sl])
                S("y", ["WU0", "WU1"][sl], 16, lambda e, ht=ht, sl=sl: e.dma_start(
                    r3(wul[sl][:], 128),
                    wub[ht, :, :, :].rearrange("a p b -> p a b")))
                W("s", ["WG0", "WG1"][sl])
                WT("s", "T", wgs_use[sl])
                S("s", "S", 1, lambda e, sl=sl: e.sign(wgl[sl][:], wgl[sl][:]))
                W("s", ["WU0", "WU1"][sl])
                WT("s", "T", wus_use[sl])
                su_rec[ht] = S("s", "S", 1, lambda e, sl=sl: e.sign(wul[sl][:], wul[sl][:]))

            gu_load_sign(0)
            for ht in range(NHT):
                sl = ht % 2
                psg = ps4a[:, sl * 1024: sl * 1024 + tw]
                psu = ps2a if sl == 0 else ps2b
                gsl = gsil if sl == 0 else gsilB
                WT("t", "S", su_rec[ht])
                if ht == 0:
                    W("t", "Q4")
                need("t", "p4g%d" % sl, "p4d%d" % sl, "ps2a" if sl == 0 else "ps2b")
                for dt in range(NT):
                    for n in range(nmm):
                        w0 = n * 512
                        w1 = min(tw, w0 + 512)
                        S("t", None, 1, lambda e, dt=dt, w0=w0, w1=w1, sl=sl, psg=psg: e.matmul(
                            psg[:, w0:w1],
                            wgl[sl][:, dt * 128:(dt + 1) * 128],
                            bass.AP(oaA, XQC_OFF + dt * PSL + w0,
                                    [[NT * T_TOK, 128], [1, w1 - w0]]),
                            start=(dt == 0), stop=(dt == NT - 1)))
                        S("t", "T" if (dt == NT - 1 and n == nmm - 1) else None, 1,
                          lambda e, dt=dt, w0=w0, w1=w1, sl=sl, psu=psu: e.matmul(
                              psu[:, w0:w1],
                              wul[sl][:, dt * 128:(dt + 1) * 128],
                              bass.AP(oaA, XQC_OFF + dt * PSL + w0,
                                      [[NT * T_TOK, 128], [1, w1 - w0]]),
                              start=(dt == 0), stop=(dt == NT - 1)))
                wgs_use[sl] = C["T"]
                wus_use[sl] = C["T"]
                if ht + 1 < NHT:
                    gu_load_sign(ht + 1)
                W("s", "T")
                need("s", "gsl%d" % sl)
                S("s", "S", 1, lambda e, tw=tw, psg=psg, gsl=gsl: e.activation(
                    gsl[:, 0:tw], psg[:, 0:tw], AF.Silu))
                done("p4g%d" % sl, "S")
                sil = C["S"]
                WT("v", "S", sil)
                S("v", "V", 1, lambda e, ht=ht, tw=tw, psu=psu, gsl=gsl: e.tensor_mul(
                    p_slab(ht)[:, 0:tw], psu[:, 0:tw], gsl[:, 0:tw]))
                done("ps2a" if sl == 0 else "ps2b", "V")
                done("gsl%d" % sl, "V")
            done("pfull", "V")
            gu_T = C["T"]

            tag(f"P4.dn{tc}")
            sd_rec = {}

            def dn_load_sign(G, tc=tc):
                dt_, g_ = G // 4, G % 4
                hsl = g_ % 2
                if G == 0 and tc == 0:
                    W("g", "PP")
                WT("g", "T", wds_use[hsl])
                S("g", ["FD0", "FD1"][hsl], 16,
                  lambda e, dt_=dt_, g_=g_, hsl=hsl: e.dma_start(
                      r3(wdl[hsl][:], 128),
                      wdb[dt_, g_ * 16:(g_ + 1) * 16, :, :].rearrange("a p b -> p a b")))
                W("s", ["FD0", "FD1"][hsl])
                WT("s", "T", wds_use[hsl])
                sd_rec[G] = S("s", "S", 1, lambda e, hsl=hsl: e.sign(
                    wdl[hsl][:], wdl[hsl][:]))

            dn_load_sign(0)
            for G in range(NT * 4):
                dt, g = G // 4, G % 4
                hsl = g % 2
                psd = ps4a[:, (dt % 2) * 1024:(dt % 2) * 1024 + tw]
                if g == 0:
                    if dt < 2:
                        need("t", "pfull", "p4g%d" % (dt % 2))
                    need("t", "p4d%d" % (dt % 2))
                WT("t", "S", sd_rec[G])
                for hidx in range(16):
                    ht = g * 16 + hidx
                    for n in range(nmm):
                        w0 = n * 512
                        w1 = min(tw, w0 + 512)
                        S("t", "T" if (hidx == 15 and n == nmm - 1) else None, 1,
                          lambda e, ht=ht, hidx=hidx, hsl=hsl, w0=w0, w1=w1, psd=psd: e.matmul(
                              psd[:, w0:w1],
                              wdl[hsl][:, hidx * 128:(hidx + 1) * 128],
                              p_slab(ht)[:, w0:w1],
                              start=(ht == 0), stop=(ht == NHT - 1)))
                wds_use[hsl] = C["T"]
                if G + 1 < NT * 4:
                    dn_load_sign(G + 1)
                if g == 3:
                    W("s", "T")
                    S("s", "S", 1, lambda e, dt=dt, tw=tw, psd=psd: e.activation(
                        bass.AP(oaA, GF_OFF + dt * PSL, [[NT * T_TOK, 128], [1, tw]]),
                        psd, AF.Silu))
                    done("p4d%d" % (dt % 2), "S")
                    done("gf%d" % dt, "S")

            ntt = tw // 128
            for ttl in range(ntt):
                tt = (t0 // 128) + ttl
                WT("y", "V", cx_free)
                S("y", "CX0", 16, lambda e, tt=tt: e.dma_start(
                    f32a[:], x_d[tt * 128:(tt + 1) * 128, :]))
                WT("y", "V", cs_free)
                S("y", "CS0", 16, lambda e, tt=tt: e.dma_start(
                    oa(SAT_OFF, 2048), sat_sp[tt, :, :]))
                for grp in range(4):
                    pp = ps2a if grp % 2 == 0 else ps2b
                    ppres = "ps2a" if grp % 2 == 0 else "ps2b"
                    need("t", ppres)
                    for j in range(4):
                        dt = grp * 4 + j
                        need("t", "gf%d" % dt)
                        S("t", "T" if j == 3 else None, 1,
                          lambda e, dt=dt, j=j, pp=pp, ttl=ttl: e.matmul(
                              pp[:, j * 128:(j + 1) * 128],
                              bass.AP(oaA, GF_OFF + dt * PSL + ttl * 128,
                                      [[NT * T_TOK, 128], [1, 128]]),
                              cIb[:], start=True, stop=True))
                    W("v", "T")
                    S("v", "V", 1, lambda e, grp=grp, pp=pp: e.tensor_copy(
                        oa(GT_OFF + grp * 512, 512), pp[:, 0:512]))
                    done(ppres, "V")
                W("v", "CX0", "CS0")
                WT("v", "CO0", co_free)
                cs_free = S("v", "V", 1, lambda e: e.tensor_mul(
                    okn[:], oa(GT_OFF, 2048), oa(SAT_OFF, 2048)))
                cx_free = S("v", "V", 1, lambda e: e.tensor_add(
                    f32b[:], okn[:], f32a[:]))
                W("y", "V")
                co_free = S("y", "CO0", 16, lambda e, tt=tt: e.dma_start(
                    out_d[tt * 128:(tt + 1) * 128, :], f32b[:]))

        W("y", "CO0")
        if DEBUG:
            W("y", "YC")

    with es, nc.Block() as block:
        @block.tensor
        def _(t):
            prog("t", t)

        @block.vector
        def _(v):
            prog("v", v)

        @block.scalar
        def _(s):
            prog("s", s)

        @block.gpsimd
        def _(g):
            prog("g", g)

        @block.sync
        def _(y):
            prog("y", y)

    return nc


def host_pack(inputs):
    x = np.ascontiguousarray(np.asarray(inputs["L0_attn_hidden_states"], dtype=np.float32))
    Wq = np.asarray(inputs["L0_attn_Wq"])
    Wk = np.asarray(inputs["L0_attn_Wk"])
    Wv = np.asarray(inputs["L0_attn_Wv"])
    Wg = np.asarray(inputs["L0_attn_Wg"])
    Ws = np.asarray(inputs["L0_attn_Ws"])
    Wo = np.asarray(inputs["L0_attn_Wo"])
    gnw = np.ascontiguousarray(np.asarray(inputs["L0_attn_g_norm_weight"], dtype=np.float32).reshape(128, 1))
    Wgf = np.asarray(inputs["L0_ffn_W_g"])
    Wuf = np.asarray(inputs["L0_ffn_W_u"])
    Wdf = np.asarray(inputs["L0_ffn_W_d"])

    fts = []
    for p in range(8):
        h0, h1 = 2 * p, 2 * p + 1
        fts.append(np.concatenate([Wq[:, h0 * 64:(h0 + 1) * 64], Wq[:, h1 * 64:(h1 + 1) * 64]], 1))
        fts.append(np.concatenate([Wk[:, h0 * 64:(h0 + 1) * 64], Wk[:, h1 * 64:(h1 + 1) * 64]], 1))
        fts.append(np.concatenate([Ws[:, h0 * 64:(h0 + 1) * 64], Ws[:, h1 * 64:(h1 + 1) * 64]], 1))
        fts.append(Wv[:, h0 * 128:(h0 + 1) * 128])
        fts.append(Wv[:, h1 * 128:(h1 + 1) * 128])
        fts.append(Wg[:, h0 * 128:(h0 + 1) * 128])
        fts.append(Wg[:, h1 * 128:(h1 + 1) * 128])
    attn_w = np.ascontiguousarray(np.stack(fts).reshape(56, 16, 128, 128))
    wo_w = np.ascontiguousarray(Wo.reshape(16, 128, D))
    wg_s = np.ascontiguousarray(Wgf.reshape(64, 128, 16, 128).transpose(0, 2, 3, 1))
    wu_s = np.ascontiguousarray(Wuf.reshape(64, 128, 16, 128).transpose(0, 2, 3, 1))
    wd_s = np.ascontiguousarray(Wdf.reshape(16, 128, 64, 128).transpose(0, 2, 3, 1))
    ident_bf = np.eye(128, dtype=ml_dtypes.bfloat16)
    ident_f32 = np.eye(128, dtype=np.float32)
    triu_bf = np.triu(np.ones((128, 128), np.float32)).astype(ml_dtypes.bfloat16)

    shared = dict(attn_w=attn_w, wo_w=wo_w, wg_s=wg_s, wu_s=wu_s, wd_s=wd_s,
                  gnw=gnw, ident_bf=ident_bf, ident_f32=ident_f32,
                  triu_bf=triu_bf)
    return x, shared


def kernel(**inputs):
    x, shared = host_pack(inputs)
    nc = build_nc()
    in_maps = [dict(shared, x=np.ascontiguousarray(x[c])) for c in range(8)]
    res = run_bass_kernel_spmd(nc, in_maps, core_ids=list(range(8)))
    return np.stack([np.asarray(res.results[c]["out"]) for c in range(8)], axis=0)


# version 06
# speedup vs baseline: 1.3786x; 1.1677x over previous
import numpy as np
import ml_dtypes
import concourse.bass as bass
import concourse.mybir as mybir
from concourse.bass_utils import run_bass_kernel_spmd

BF = mybir.dt.bfloat16
F32 = mybir.dt.float32
I32 = mybir.dt.int32
U8 = mybir.dt.uint8
AF = mybir.ActivationFunctionType
ALU = mybir.AluOpType
AX = mybir.AxisListType

T_TOK = 2048
D = 2048
NT = 16
NP = 8
NCH = 16
NHT = 64
TCH = [768, 768, 512]
TOFF = [0, 768, 1536]
PSL = 768

DEBUG = False
TAGS = []


def build_nc():
    nc = bass.Bass("TRN2", target_bir_lowering=False,
                   detect_race_conditions=False)

    x_d = nc.dram_tensor("x", [T_TOK, D], F32, kind="ExternalInput")
    aw_d = nc.dram_tensor("attn_w", [56, 16, 128, 128], F32, kind="ExternalInput")
    wo_d = nc.dram_tensor("wo_w", [16, 128, D], F32, kind="ExternalInput")
    wg_d = nc.dram_tensor("wg_s", [64, 16, 128, 128], F32, kind="ExternalInput")
    wu_d = nc.dram_tensor("wu_s", [64, 16, 128, 128], F32, kind="ExternalInput")
    wd_d = nc.dram_tensor("wd_s", [16, 64, 128, 128], F32, kind="ExternalInput")
    gnw_d = nc.dram_tensor("gnw", [128, 1], F32, kind="ExternalInput")
    ib_d = nc.dram_tensor("ident_bf", [128, 128], BF, kind="ExternalInput")
    if_d = nc.dram_tensor("ident_f32", [128, 128], F32, kind="ExternalInput")
    tri_d = nc.dram_tensor("triu_bf", [128, 128], BF, kind="ExternalInput")
    out_d = nc.dram_tensor("out", [T_TOK, D], F32, kind="ExternalOutput")
    xq_sp = nc.dram_tensor("xq_sp", [NT, 128, T_TOK], BF, kind="Internal")
    wgb = nc.dram_tensor("wgb", [64, 16, 128, 128], BF, kind="Internal")
    wub = nc.dram_tensor("wub", [64, 16, 128, 128], BF, kind="Internal")
    wdb = nc.dram_tensor("wdb", [16, 64, 128, 128], BF, kind="Internal")
    sat_sp = nc.dram_tensor("sat_sp", [NT, 128, D], BF, kind="Internal")

    dbg = {}
    if DEBUG:
        dbg["xnT"] = nc.dram_tensor("dbg_xnT", [NT, 128, T_TOK], BF, kind="ExternalOutput")
        dbg["xq"] = nc.dram_tensor("dbg_xq", [NT, 128, T_TOK], BF, kind="ExternalOutput")
        dbg["oall"] = nc.dram_tensor("dbg_oall", [16, 128, T_TOK], BF, kind="ExternalOutput")
        dbg["sat"] = nc.dram_tensor("dbg_sat", [NT, 128, D], BF, kind="ExternalOutput")
        dbg["st"] = nc.dram_tensor("dbg_st", [128, 6 * NT], F32, kind="ExternalOutput")

    from contextlib import ExitStack
    es = ExitStack()

    def sbuf(name, shape, dt):
        return es.enter_context(nc.sbuf_tensor(name, shape, dt))

    xnA = sbuf("xnA", [128, NT * T_TOK], BF)
    oaA = sbuf("oaA", [128, NT * T_TOK], BF)
    GF_OFF = 0
    XQC_OFF = 12288
    GT_OFF = 24576
    SAT_OFF = 26624
    POV_OFF = 28672

    cIb = sbuf("cIb", [128, 128], BF)
    cIf = sbuf("cIf", [128, 128], F32)
    cTri = sbuf("cTri", [128, 128], BF)
    cGnw = sbuf("cGnw", [128, 1], F32)
    cZ = sbuf("cZ", [128, 1], F32)

    stats = {}
    for nm in ("s1", "s2", "m", "r", "nb", "s2q", "amax", "r2", "sq", "inv",
               "sq2r", "t1"):
        stats[nm] = sbuf("st_" + nm, [128, NT], F32)
    smx = sbuf("smx", [128, 2 * NCH], F32)
    ssm = sbuf("ssm", [128, 2 * NCH], F32)
    srs = sbuf("srs", [128, 2 * NCH], F32)
    sos = sbuf("sos", [128, 2 * NCH], F32)
    sor = sbuf("sor", [128, 2 * NCH], F32)

    f32a = sbuf("f32a", [128, 2048], F32)
    f32b = sbuf("f32b", [128, 2048], F32)
    okn = sbuf("okn", [128, 2048], F32)

    AR = 54272
    arena = sbuf("arena", [128, AR], U8)
    ar_base = nc.lookup_mloc(arena).addr

    def at(name, off, shape, dt):
        return nc.alloc_sbuf_tensor_at(name, shape, dt, offset=ar_base + off)

    qqT = at("qqT", 0, [128, 2048], BF)
    kkpw = at("kkpw", 4096, [128, 2048], BF)
    embf = at("embf", 8192, [128, 2048], BF)
    etok = at("etok", 12288, [128, 2048], BF)
    ktok = at("ktok", 16384, [128, 2048], BF)
    vt0 = at("vt0", 20480, [128, 2048], BF)
    vt1 = at("vt1", 24576, [128, 2048], BF)
    gs0 = at("gs0", 28672, [128, 2048], BF)
    gs1 = at("gs1", 32768, [128, 2048], BF)
    sm0 = at("sm0", 36864, [128, 2048], BF)
    sm1 = at("sm1", 40960, [128, 2048], BF)
    pwT = at("pwT", 45056, [128, 2048], BF)
    scr4 = at("scr4", 49152, [128, 1024], F32)
    xqi = at("xqi", 20480, [128, 2048], I32)
    pext = at("pext", 0, [128, 17 * PSL], BF)
    wgl = [at("wgl0", 26624, [128, 2048], BF), at("wgl1", 30720, [128, 2048], BF)]
    wul = [at("wul0", 34816, [128, 2048], BF), at("wul1", 38912, [128, 2048], BF)]
    wdl = [at("wdl0", 43008, [128, 2048], BF), at("wdl1", 47104, [128, 2048], BF)]
    dsl = [at("dslA", 26624, [128, 4096], BF), at("dslB", 34816, [128, 4096], BF),
           at("dslC", 43008, [128, 4096], BF)]
    gsil = at("gsil", 51200, [128, 768], BF)
    gsilB = at("gsilB", 52736, [128, 768], BF)

    ps4a = es.enter_context(nc.psum_tensor("ps4a", [128, 2048], F32))
    ps2a = es.enter_context(nc.psum_tensor("ps2a", [128, 1024], F32))
    ps2b = es.enter_context(nc.psum_tensor("ps2b", [128, 1024], F32))

    sem_names = ["T", "V", "S", "YC",
                 "XA", "XB", "XQ0", "XQ1",
                 "WA0", "WA1", "GWO", "SP0", "SP1",
                 "Q4", "WG0", "WG1", "WU0", "WU1", "FD0", "FD1", "FD2", "PP",
                 "CX0", "CS0", "CO0"]
    sems = {n: nc.alloc_semaphore(name="sem_" + n) for n in sem_names}

    def zbc(n):
        return bass.AP(cZ, 0, [[1, 128], [0, n]])

    def tri_bc(nch):
        return bass.AP(cTri, 0, [[128, 128], [0, nch], [1, 128]])

    def r3(ap, b):
        return ap.rearrange("p (a b) -> p a b", b=b)

    def p_slab(ht):
        if ht < 42:
            return bass.AP(xnA, ht * PSL, [[NT * T_TOK, 128], [1, PSL]])
        if ht < 47:
            return bass.AP(oaA, POV_OFF + (ht - 42) * PSL, [[NT * T_TOK, 128], [1, PSL]])
        return bass.AP(pext, (ht - 47) * PSL, [[17 * PSL, 128], [1, PSL]])

    def oa(off, n):
        return bass.AP(oaA, off, [[NT * T_TOK, 128], [1, n]])

    def prog(me, eng):
        C = {k: 0 for k in sems}
        LW = {k: 0 for k in sems}
        _tmm = [0]
        rec = {}

        def tag(name):
            if me == "t":
                TAGS.append((name, _tmm[0]))

        def done(res, engsem):
            rec[res] = (engsem, C[engsem])

        def need(who, *ress):
            for res in ress:
                e_, c_ = rec.get(res, (None, 0))
                if e_ is not None:
                    WT(who, e_, c_)

        def W(who, *names):
            if who != me:
                return
            for n_ in names:
                if C[n_] > 0 and LW[n_] < C[n_]:
                    eng.wait_ge(sems[n_], C[n_])
                    LW[n_] = C[n_]

        def WT(who, sem, val):
            if who == me and val > 0 and LW[sem] < val:
                eng.wait_ge(sems[sem], val)
                LW[sem] = val

        def S(owner, sem, n, f):
            inst = f(eng) if owner == me else None
            if owner == "t":
                _tmm[0] += 1
            if sem is not None:
                C[sem] += n
                if inst is not None:
                    inst.then_inc(sems[sem], n)
            return C[sem] if sem is not None else None

        S("y", "YC", 16, lambda e: e.dma_start(cIb[:], ib_d[:, :]))
        S("y", "YC", 16, lambda e: e.dma_start(cIf[:], if_d[:, :]))
        S("y", "YC", 16, lambda e: e.dma_start(cTri[:], tri_d[:, :]))
        S("y", "YC", 16, lambda e: e.dma_start(cGnw[:], gnw_d[:, :]))
        W("v", "YC")
        S("v", "V", 1, lambda e: e.memset(cZ[:], 0.0))

        pp_jobs = []
        for k in range(8):
            pp_jobs.append(lambda e, k=k: e.dma_start(
                wgb[k * 8:(k + 1) * 8, :, :, :], wg_d[k * 8:(k + 1) * 8, :, :, :]))
            pp_jobs.append(lambda e, k=k: e.dma_start(
                wub[k * 8:(k + 1) * 8, :, :, :], wu_d[k * 8:(k + 1) * 8, :, :, :]))
            pp_jobs.append(lambda e, k=k: e.dma_start(
                wdb[k * 2:(k + 1) * 2, :, :, :], wd_d[k * 2:(k + 1) * 2, :, :, :]))

        tag("P1")
        xslot = [f32a, f32b]
        xsem = ["XA", "XB"]
        xfree = [0, 0]
        xfree_s = [0, 0]
        qstg = [etok, ktok]
        qsem = ["XQ0", "XQ1"]
        qfree = [0, 0]

        def stv(name, f):
            W("v", "V", "S")
            S("v", "V", 1, f)

        for i in range(NT):
            sl = i % 2
            xt = xslot[sl]
            WT("y", "V", xfree[sl])
            WT("y", "S", xfree_s[sl])
            S("y", xsem[sl], 16, lambda e, i=i, xt=xt: e.dma_start(
                xt[:], x_d[i * 128:(i + 1) * 128, :]))
            W("s", xsem[sl])
            xfree_s[sl] = S("s", "S", 1, lambda e, i=i, xt=xt: e.activation(
                oa(15 * T_TOK, 2048), xt[:], AF.Square,
                accum_out=stats["s2"][:, i:i + 1]))
            W("v", xsem[sl])
            xfree[sl] = S("v", "V", 1, lambda e, i=i, xt=xt: e.reduce_sum(
                stats["s1"][:, i:i + 1], xt[:], axis=AX.X))
        W("v", "S")
        stv("m", lambda e: e.tensor_scalar_mul(stats["m"][:, :], stats["s1"][:, :], 1.0 / D))
        stv("t1", lambda e: e.tensor_mul(stats["t1"][:, :], stats["m"][:, :], stats["m"][:, :]))
        stv("v1", lambda e: e.tensor_scalar_mul(stats["r"][:, :], stats["s2"][:, :], 1.0 / D))
        stv("v2", lambda e: e.tensor_sub(stats["r"][:, :], stats["r"][:, :], stats["t1"][:, :]))
        stv("v3", lambda e: e.tensor_scalar_add(stats["r"][:, :], stats["r"][:, :], 1e-5))
        W("s", "V")
        S("s", "S", 1, lambda e: e.activation(stats["r"][:, :], stats["r"][:, :], AF.Sqrt))
        W("v", "S")
        stv("v4", lambda e: e.reciprocal(stats["r"][:, :], stats["r"][:, :]))
        stv("v5", lambda e: e.tensor_mul(stats["nb"][:, :], stats["m"][:, :], stats["r"][:, :]))
        stv("v6", lambda e: e.tensor_scalar_mul(stats["nb"][:, :], stats["nb"][:, :], -1.0))
        done("LN", "V")

        for i in range(NT):
            sl = i % 2
            xt = xslot[sl]
            WT("y", "V", xfree[sl])
            WT("y", "S", xfree_s[sl])
            S("y", xsem[sl], 16, lambda e, i=i, xt=xt: e.dma_start(
                xt[:], x_d[i * 128:(i + 1) * 128, :]))
            W("v", xsem[sl])
            need("v", "LN")
            W("v", "S")
            S("v", "V", 1, lambda e, i=i, xt=xt: e.scalar_tensor_tensor(
                okn[:], xt[:], stats["r"][:, i:i + 1],
                bass.AP(stats["nb"], i, [[NT, 128], [0, 2048]]),
                ALU.mult, ALU.add))
            xfree[sl] = C["V"]
            xfree_s[sl] = 0
            W("s", "V")
            S("s", "S", 1, lambda e, i=i: e.activation(
                oa(15 * T_TOK, 2048), okn[:], AF.Square,
                accum_out=stats["s2q"][:, i:i + 1]))
            S("v", "V", 1, lambda e, i=i: e.reduce_max(
                stats["amax"][:, i:i + 1], okn[:], axis=AX.X,
                apply_absolute_value=True))
            W("v", "T")
            S("v", "V", 1, lambda e: e.tensor_copy(sm0[:], okn[:]))
            W("t", "YC")
            for grp in range(4):
                pp = ps2a if grp % 2 == 0 else ps2b
                W("t", "V", "S")
                for j in range(4):
                    dt = grp * 4 + j
                    S("t", "T" if j == 3 else None, 1, lambda e, dt=dt, j=j, pp=pp: e.matmul(
                        pp[:, j * 128:(j + 1) * 128],
                        sm0[:, dt * 128:(dt + 1) * 128], cIb[:],
                        start=True, stop=True))
                W("s", "T")
                S("s", "S", 1, lambda e, grp=grp, pp=pp, i=i: e.copy(
                    bass.AP(xnA, (grp * 4) * T_TOK + i * 128,
                            [[NT * T_TOK, 128], [T_TOK, 4], [1, 128]]),
                    pp[:, 0:512]))
            done("xnT%d" % i, "S")
        W("v", "S")
        stv("q1", lambda e: e.tensor_scalar(stats["r2"][:, :], stats["s2q"][:, :],
                                            1.0 / D, 1e-8, ALU.mult, ALU.add))
        W("s", "V")
        S("s", "S", 1, lambda e: e.activation(stats["r2"][:, :], stats["r2"][:, :], AF.Sqrt))
        W("v", "S")
        stv("q2", lambda e: e.reciprocal(stats["r2"][:, :], stats["r2"][:, :]))
        stv("q3", lambda e: e.tensor_mul(stats["amax"][:, :], stats["amax"][:, :], stats["r2"][:, :]))
        stv("q4", lambda e: e.reciprocal(stats["sq"][:, :], stats["amax"][:, :]))
        stv("q5", lambda e: e.tensor_scalar_mul(stats["sq"][:, :], stats["sq"][:, :], 127.0))
        stv("q6", lambda e: e.reciprocal(stats["inv"][:, :], stats["sq"][:, :]))
        stv("q7", lambda e: e.tensor_mul(stats["sq2r"][:, :], stats["sq"][:, :], stats["r2"][:, :]))
        stv("q8", lambda e: e.tensor_mul(stats["t1"][:, :], stats["r"][:, :], stats["sq2r"][:, :]))
        stv("q9", lambda e: e.tensor_mul(stats["s1"][:, :], stats["nb"][:, :], stats["sq2r"][:, :]))
        done("QS", "V")

        for i in range(NT):
            sl = i % 2
            xt = xslot[sl]
            WT("y", "V", xfree[sl])
            S("y", xsem[sl], 16, lambda e, i=i, xt=xt: e.dma_start(
                xt[:], x_d[i * 128:(i + 1) * 128, :]))
            W("v", xsem[sl])
            need("v", "QS")
            S("v", "V", 1, lambda e, i=i, xt=xt: e.scalar_tensor_tensor(
                okn[:], xt[:], stats["t1"][:, i:i + 1],
                bass.AP(stats["s1"], i, [[NT, 128], [0, 2048]]),
                ALU.mult, ALU.add))
            xfree[sl] = C["V"]
            S("v", "V", 1, lambda e: e.tensor_copy(xqi[:], okn[:]))
            W("v", "V", "T")
            S("v", "V", 1, lambda e, i=i: e.tensor_scalar_mul(
                sm1[:], xqi[:], stats["inv"][:, i:i + 1]))
            qsl = i % 2
            stg = qstg[qsl]
            for grp in range(4):
                pp = ps2a if grp % 2 == 0 else ps2b
                W("t", "V", "S")
                WT("t", qsem[qsl], qfree[qsl])
                for j in range(4):
                    dt = grp * 4 + j
                    S("t", "T" if j == 3 else None, 1, lambda e, dt=dt, j=j, pp=pp: e.matmul(
                        pp[:, j * 128:(j + 1) * 128],
                        sm1[:, dt * 128:(dt + 1) * 128], cIb[:],
                        start=True, stop=True))
                W("s", "T")
                S("s", "S", 1, lambda e, grp=grp, pp=pp, stg=stg: e.copy(
                    stg[:, grp * 512:(grp + 1) * 512], pp[:, 0:512]))
            W("y", "S")
            qfree[qsl] = S("y", qsem[qsl], 16, lambda e, i=i, stg=stg: e.dma_start(
                xq_sp[:, :, i * 128:(i + 1) * 128].rearrange("a p b -> p a b"),
                r3(stg[:], 128)))

        if DEBUG:
            W("y", "S")
            S("y", "YC", 16, lambda e: e.dma_start(
                dbg["xnT"][:, :, :].rearrange("a p b -> p a b"),
                r3(xnA[:], T_TOK)))
            W("y", "XQ0", "XQ1")
            S("y", "YC", 16, lambda e: e.dma_start(
                dbg["xq"][:, :, :], xq_sp[:, :, :]))

        tag("P2")
        wfree = [0, 0]

        def wp_ap(p, par, tile=None):
            off = (2 * p + par) * T_TOK
            if tile is None:
                return bass.AP(oaA, off, [[NT * T_TOK, 128], [1, 2048]])
            return bass.AP(oaA, off + tile * 128, [[NT * T_TOK, 128], [1, 128]])

        def tgroup(src, dst, fp32, src_res=None, dst_res=None, who_evac="s"):
            esem = "V" if who_evac == "v" else "S"
            for grp in range(4):
                pp = ps2a if grp % 2 == 0 else ps2b
                ppres = "ps2a" if grp % 2 == 0 else "ps2b"
                if src_res:
                    need("t", src_res)
                need("t", ppres)
                for j in range(4):
                    c = grp * 4 + j
                    if fp32:
                        S("t", "T" if j == 3 else None, 1,
                          lambda e, c=c, j=j, pp=pp, src=src: e.transpose(
                              pp[:, j * 128:(j + 1) * 128],
                              src[:, c * 128:(c + 1) * 128], cIf[:]))
                    else:
                        S("t", "T" if j == 3 else None, 1,
                          lambda e, c=c, j=j, pp=pp, src=src: e.matmul(
                              pp[:, j * 128:(j + 1) * 128],
                              src[:, c * 128:(c + 1) * 128], cIb[:],
                              start=True, stop=True))
                W(who_evac, "T")
                if who_evac == "s":
                    S("s", "S", 1, lambda e, grp=grp, pp=pp, dst=dst: e.copy(
                        dst[:, grp * 512:(grp + 1) * 512], pp[:, 0:512]))
                else:
                    S("v", "V", 1, lambda e, grp=grp, pp=pp, dst=dst: e.tensor_copy(
                        dst[:, grp * 512:(grp + 1) * 512], pp[:, 0:512]))
                done(ppres, esem)
            if dst_res:
                done(dst_res, esem)

        def wp_load(p, f):
            par = f % 2
            WT("g", "T", wfree[par])
            S("g", ["WA0", "WA1"][par], 16, lambda e, p=p, f=f, par=par: e.dma_start(
                r3(wp_ap(p, par), 128),
                aw_d[p * 7 + f, :, :, :].rearrange("a p b -> p a b")))

        for p in range(NP):
            tag(f"P2.proj")
            wfree[0] = wfree[1] = 0
            wp_load(p, 0)
            wp_load(p, 1)
            for _ in range(3):
                if pp_jobs:
                    S("g", "PP", 16, pp_jobs.pop(0))
            for f in range(7):
                par = f % 2
                W("t", ["WA0", "WA1"][par])
                for n in range(4):
                    if p == 0:
                        need("t", "xnT%d" % (4 * n + 3))
                    need("t", "psA%d" % n)
                    for dt in range(NT):
                        S("t", "T" if dt == NT - 1 else None, 1,
                          lambda e, n=n, dt=dt, p=p, par=par: e.matmul(
                              ps4a[:, n * 512:(n + 1) * 512],
                              wp_ap(p, par, dt),
                              bass.AP(xnA, dt * T_TOK + n * 512, [[NT * T_TOK, 128], [1, 512]]),
                              start=(dt == 0), stop=(dt == NT - 1)))
                    if n == 3:
                        wfree[par] = C["T"]
                        if f + 2 < 7:
                            wp_load(p, f + 2)
                    pc = ps4a[:, n * 512:(n + 1) * 512]
                    if f == 2:
                        W("v", "T")
                        S("v", "V", 1, lambda e, n=n, pc=pc: e.tensor_scalar(
                            f32a[:, n * 512:(n + 1) * 512], pc,
                            -32.0, 32.0, ALU.max, ALU.min))
                        done("psA%d" % n, "V")
                    else:
                        W("s", "T")
                        if f == 0:
                            S("s", "S", 1, lambda e, n=n, pc=pc: e.mul(
                                qqT[:, n * 512:(n + 1) * 512], pc, 0.125))
                        elif f == 1:
                            S("s", "S", 1, lambda e, n=n, pc=pc: e.copy(
                                kkpw[:, n * 512:(n + 1) * 512], pc))
                        elif f in (3, 4):
                            vtb = vt0 if f == 3 else vt1
                            S("s", "S", 1, lambda e, n=n, pc=pc, vtb=vtb: e.copy(
                                vtb[:, n * 512:(n + 1) * 512], pc))
                        else:
                            gsb = gs0 if f == 5 else gs1
                            S("s", "S", 1, lambda e, n=n, pc=pc, gsb=gsb: e.activation(
                                gsb[:, n * 512:(n + 1) * 512], pc, AF.Silu))
                        done("psA%d" % n, "S")
                if f == 0:
                    done("qqT", "S")
                elif f == 1:
                    done("kkT", "S")
                elif f == 2:
                    done("sclip", "V")
                elif f == 3:
                    done("vt0", "S")
                elif f == 4:
                    done("vt1", "S")
                elif f == 5:
                    done("gs0", "S")
                elif f == 6:
                    done("gs1", "S")
                if f == 2:
                    need("s", "sclip")
                    S("s", "S", 1, lambda e: e.activation(f32b[:], f32a[:], AF.Exp))
                    done("E32", "S")
                    need("v", "E32")
                    S("v", "V", 1, lambda e: e.tensor_tensor_scan(
                        f32a[:], f32b[:], zbc(2048), 0.0, ALU.add, ALU.add))
                    S("v", "V", 1, lambda e: e.tensor_copy(embf[:], f32b[:]))
                    done("embf", "V")
                    S("v", "V", 1, lambda e: e.reciprocal(f32b[:], f32a[:]))
                    done("recipAm", "V")

            tag("P2.tr")
            if p == 0:
                W("s", "XQ0", "XQ1")
            tgroup(kkpw, ktok, False, "kkT", "Ktok")
            for vtb, stg, rr in ((vt0, sm0, "vt0"), (vt1, sm1, "vt1")):
                tgroup(vtb, stg, False, rr, rr + "st")
                need("v", rr + "st")
                S("v", "V", 1, lambda e, vtb=vtb, stg=stg: e.tensor_copy(vtb[:], stg[:]))
                done(rr + "tok", "V")
            tgroup(embf, etok, False, "embf", "etok")
            tgroup(f32b, f32a, True, "recipAm", "recipA")

            tag("P2.pk")
            for rnd in range(2):
                need("t", "Ktok", "etok", "ps2b")
                if rnd == 1:
                    W("t", "V")
                for cc in range(8):
                    c = rnd * 8 + cc
                    S("t", "T" if cc == 7 else None, 1, lambda e, c=c, cc=cc: e.matmul(
                        ps2b[:, cc * 128:(cc + 1) * 128],
                        ktok[:, c * 128:(c + 1) * 128],
                        etok[:, c * 128:(c + 1) * 128],
                        start=True, stop=True))
                W("v", "T")
                if rnd == 0:
                    S("v", "V", 1, lambda e: e.memset(f32b[:, 0:128], 0.0))
                for cc in range(8):
                    c = rnd * 8 + cc
                    if c == NCH - 1:
                        break
                    S("v", "V", 1, lambda e, c=c, cc=cc: e.tensor_add(
                        f32b[:, (c + 1) * 128:(c + 2) * 128],
                        f32b[:, c * 128:(c + 1) * 128],
                        ps2b[:, cc * 128:(cc + 1) * 128]))
            S("v", "V", 1, lambda e: e.tensor_copy(ktok[:], f32b[:]))
            done("Hk", "V")
            done("ps2b", "V")

            tag("P2.sok")
            for h in range(2):
                smh = sm0 if h == 0 else sm1
                pso = ps2a if h == 0 else ps2b
                okh = okn[:, h * 1024:(h + 1) * 1024]
                for c in range(NCH):
                    if c % 4 == 0:
                        need("t", "kkT", "qqT", "psA%d" % (c // 4))
                    S("t", "T" if c % 4 == 3 else None, 1, lambda e, c=c, h=h: e.matmul(
                        ps4a[:, c * 128:(c + 1) * 128],
                        kkpw[h * 64:(h + 1) * 64, c * 128:(c + 1) * 128],
                        qqT[h * 64:(h + 1) * 64, c * 128:(c + 1) * 128],
                        start=True, stop=True))
                    if c % 4 == 3:
                        W("v", "T")
                        g4 = c // 4
                        S("v", "V", 1, lambda e, g4=g4, smh=smh: e.tensor_tensor(
                            r3(smh[:, g4 * 512:(g4 + 1) * 512], 128),
                            r3(ps4a[:, g4 * 512:(g4 + 1) * 512], 128),
                            tri_bc(4), ALU.mult))
                        done("psA%d" % g4, "V")
                        done("smm%d_%d" % (h, g4), "V")
                need("t", "etok", "Hk", "ps2a" if h == 0 else "ps2b")
                for c in range(NCH):
                    last = (c == NCH - 1)
                    if c % 4 == 0:
                        need("t", "smm%d_%d" % (h, c // 4))
                    S("t", None, 1, lambda e, c=c, h=h, smh=smh, pso=pso: e.matmul(
                        pso[:, c * 64:(c + 1) * 64],
                        smh[:, c * 128:(c + 1) * 128],
                        etok[:, c * 128 + h * 64: c * 128 + h * 64 + 64],
                        start=True, stop=(c == 0)))
                    if c > 0:
                        S("t", "T" if last else None, 1, lambda e, c=c, h=h: e.matmul(
                            pso[:, c * 64:(c + 1) * 64],
                            qqT[h * 64:(h + 1) * 64, c * 128:(c + 1) * 128],
                            ktok[h * 64:(h + 1) * 64, c * 128 + h * 64: c * 128 + h * 64 + 64],
                            start=False, stop=True))
                W("v", "T")
                need("v", "recipA")
                S("v", "V", 1, lambda e, okh=okh, h=h, pso=pso: e.tensor_tensor(
                    r3(okh, 64), r3(pso[:], 64),
                    bass.AP(f32a, h * 64, [[2048, 128], [128, NCH], [1, 64]]),
                    ALU.mult))
                done("ps2a" if h == 0 else "ps2b", "V")
                S("v", "V", 1, lambda e, okh=okh, h=h: e.reduce_max(
                    smx[:, h * NCH:(h + 1) * NCH], r3(okh, 64),
                    axis=AX.X, negate=True))
                done("smx%d" % h, "V")
            for h in range(2):
                okh = okn[:, h * 1024:(h + 1) * 1024]
                need("s", "smx%d" % h)
                if h == 1:
                    W("s", "V")
                for c in range(NCH):
                    S("s", "S" if c == NCH - 1 else None, 1, lambda e, c=c, okh=okh, h=h: e.activation(
                        scr4[:, c * 64:(c + 1) * 64], okh[:, c * 64:(c + 1) * 64],
                        AF.Exp, bias=smx[:, h * NCH + c:h * NCH + c + 1],
                        accum_out=ssm[:, h * NCH + c:h * NCH + c + 1]))
                done("exp%d" % h, "S")
                need("v", "exp%d" % h)
                W("v", "T")
                W("v", "V")
                S("v", "V", 1, lambda e, h=h: e.reciprocal(
                    srs[:, h * NCH:(h + 1) * NCH], ssm[:, h * NCH:(h + 1) * NCH]))
                W("v", "V")
                S("v", "V", 1, lambda e, okh=okh, h=h: e.tensor_tensor(
                    r3(okh, 64),
                    bass.AP(f32a, h * 64, [[2048, 128], [128, NCH], [1, 64]]),
                    bass.AP(srs, h * NCH, [[2 * NCH, 128], [1, NCH], [0, 64]]),
                    ALU.mult))
                S("v", "V", 1, lambda e, okh=okh, h=h: e.tensor_tensor(
                    bass.AP(kkpw, h * 64, [[2048, 128], [128, NCH], [1, 64]]),
                    r3(scr4[:], 64), r3(okh, 64), ALU.mult))

            done("pw", "V")
            tag("P2.pw")
            tgroup(kkpw, pwT, False, "pw", "pwT")

            for rnd in range(2):
                need("t", "etok", "vt0tok", "vt1tok", "ps2b")
                W("t", "V")
                for cc in range(8):
                    c = rnd * 8 + cc
                    S("t", None, 1, lambda e, c=c, cc=cc: e.matmul(
                        ps2b[0:64, cc * 128:(cc + 1) * 128],
                        etok[:, c * 128: c * 128 + 64],
                        vt0[:, c * 128:(c + 1) * 128],
                        start=True, stop=True))
                    S("t", "T" if cc == 7 else None, 1, lambda e, c=c, cc=cc: e.matmul(
                        ps2b[64:128, cc * 128:(cc + 1) * 128],
                        etok[:, c * 128 + 64: c * 128 + 128],
                        vt1[:, c * 128:(c + 1) * 128],
                        start=True, stop=True))
                W("v", "T")
                if rnd == 0:
                    S("v", "V", 1, lambda e: e.memset(f32b[:, 0:128], 0.0))
                for cc in range(8):
                    c = rnd * 8 + cc
                    if c == NCH - 1:
                        break
                    S("v", "V", 1, lambda e, c=c, cc=cc: e.tensor_add(
                        f32b[:, (c + 1) * 128:(c + 2) * 128],
                        f32b[:, c * 128:(c + 1) * 128],
                        ps2b[:, cc * 128:(cc + 1) * 128]))
            S("v", "V", 1, lambda e: e.tensor_copy(qqT[:], f32b[:]))
            done("Hv", "V")
            done("ps2b", "V")

            tag("P2.ov")
            for h in range(2):
                smh = sm0 if h == 0 else sm1
                vth = vt0 if h == 0 else vt1
                gsh = gs0 if h == 0 else gs1
                hh = 2 * p + h
                for c in range(NCH):
                    if c % 4 == 0:
                        need("t", "embf", "pwT", "psA%d" % (c // 4))
                    S("t", "T" if c % 4 == 3 else None, 1, lambda e, c=c, h=h: e.matmul(
                        ps4a[:, c * 128:(c + 1) * 128],
                        embf[h * 64:(h + 1) * 64, c * 128:(c + 1) * 128],
                        pwT[h * 64:(h + 1) * 64, c * 128:(c + 1) * 128],
                        start=True, stop=True))
                    if c % 4 == 3:
                        W("v", "T")
                        g4 = c // 4
                        S("v", "V", 1, lambda e, g4=g4, smh=smh: e.tensor_tensor(
                            r3(smh[:, g4 * 512:(g4 + 1) * 512], 128),
                            r3(ps4a[:, g4 * 512:(g4 + 1) * 512], 128),
                            tri_bc(4), ALU.mult))
                        done("psA%d" % g4, "V")
                        done("smm%d_%d" % (h, g4), "V")
                need("t", "Hv", "vt%dtok" % h)
                for c in range(NCH):
                    last = (c == NCH - 1)
                    if c % 4 == 0:
                        need("t", "smm%d_%d" % (h, c // 4), "psA%d" % (c // 4))
                    S("t", None, 1, lambda e, c=c, smh=smh, vth=vth: e.matmul(
                        ps4a[:, c * 128:(c + 1) * 128],
                        smh[:, c * 128:(c + 1) * 128],
                        vth[:, c * 128:(c + 1) * 128],
                        start=True, stop=(c == 0)))
                    if c > 0:
                        S("t", "T" if last else None, 1, lambda e, c=c, h=h: e.matmul(
                            ps4a[:, c * 128:(c + 1) * 128],
                            pwT[h * 64:(h + 1) * 64, c * 128:(c + 1) * 128],
                            qqT[h * 64:(h + 1) * 64, c * 128:(c + 1) * 128],
                            start=False, stop=True))
                W("s", "T")
                need("s", "Hv")
                S("s", "S", 1, lambda e: e.activation(f32b[:], ps4a[:], AF.Square))
                W("v", "S")
                S("v", "V", 1, lambda e, h=h: e.reduce_sum(
                    sos[:, h * NCH:(h + 1) * NCH], r3(f32b[:], 128), axis=AX.X))
                W("v", "V")
                S("v", "V", 1, lambda e, h=h: e.tensor_scalar(
                    sor[:, h * NCH:(h + 1) * NCH], sos[:, h * NCH:(h + 1) * NCH],
                    1.0 / 128.0, 1e-5, ALU.mult, ALU.add))
                W("s", "V")
                S("s", "S", 1, lambda e, h=h: e.activation(
                    sor[:, h * NCH:(h + 1) * NCH], sor[:, h * NCH:(h + 1) * NCH],
                    AF.Sqrt))
                W("v", "S")
                W("v", "V")
                S("v", "V", 1, lambda e, h=h: e.reciprocal(
                    sor[:, h * NCH:(h + 1) * NCH], sor[:, h * NCH:(h + 1) * NCH]))
                W("v", "V")
                S("v", "V", 1, lambda e, h=h, smh=smh: e.tensor_tensor(
                    r3(smh[:], 128), r3(ps4a[:], 128),
                    bass.AP(sor, h * NCH, [[2 * NCH, 128], [1, NCH], [0, 128]]),
                    ALU.mult))
                for grp in range(4):
                    pp = ps2a if grp % 2 == 0 else ps2b
                    W("t", "V", "S")
                    for j in range(4):
                        c = grp * 4 + j
                        S("t", "T" if j == 3 else None, 1,
                          lambda e, c=c, j=j, pp=pp, smh=smh: e.matmul(
                              pp[:, j * 128:(j + 1) * 128],
                              smh[:, c * 128:(c + 1) * 128], cIb[:],
                              start=True, stop=True))
                    W("v", "T")
                    S("v", "V", 1, lambda e, grp=grp, pp=pp, gsh=gsh, hh=hh: e.scalar_tensor_tensor(
                        oa(hh * T_TOK + grp * 512, 512),
                        pp[:, 0:512], cGnw[:, 0:1],
                        gsh[:, grp * 512:(grp + 1) * 512],
                        ALU.mult, ALU.mult))

        p2_T = C["T"]

        if DEBUG:
            W("y", "V")
            S("y", "YC", 16, lambda e: e.dma_start(
                dbg["oall"][:, :, :].rearrange("a p b -> p a b"),
                r3(oaA[:], T_TOK)))

        tag("P3")
        WT("g", "T", p2_T)
        for vt in range(16):
            S("g", "GWO", 16, lambda e, vt=vt: e.dma_start(
                bass.AP(xnA, vt * T_TOK, [[NT * T_TOK, 128], [1, D]]),
                wo_d[vt, :, :]))
        sslot = [qqT, kkpw]
        ssemn = ["SP0", "SP1"]
        sfree = [0, 0]
        for tt in range(NT):
            sl = tt % 2
            stg = sslot[sl]
            W("t", "GWO")
            W("t", "V")
            for vt in range(16):
                for n in range(4):
                    if sl == 0:
                        tgt = ps4a[:, n * 512:(n + 1) * 512]
                    else:
                        tgt = (ps2a if n < 2 else ps2b)[:, (n % 2) * 512:((n % 2) + 1) * 512]
                    S("t", "T" if (vt == 15 and n == 3) else None, 1,
                      lambda e, vt=vt, n=n, tt=tt, tgt=tgt: e.matmul(
                          tgt,
                          oa(vt * T_TOK + tt * 128, 128),
                          bass.AP(xnA, vt * T_TOK + n * 512, [[NT * T_TOK, 128], [1, 512]]),
                          start=(vt == 0), stop=(vt == 15)))
            WT("v", ssemn[sl], sfree[sl])
            W("v", "T")
            if sl == 0:
                S("v", "V", 1, lambda e, stg=stg: e.tensor_copy(stg[:], ps4a[:]))
            else:
                S("v", "V", 1, lambda e, stg=stg: e.tensor_copy(stg[:, 0:1024], ps2a[:]))
                S("v", "V", 1, lambda e, stg=stg: e.tensor_copy(stg[:, 1024:2048], ps2b[:]))
            W("y", "V")
            sfree[sl] = S("y", ssemn[sl], 16, lambda e, tt=tt, stg=stg: e.dma_start(
                sat_sp[tt, :, :], stg[:]))

        p3_T = C["T"]

        if DEBUG:
            W("y", "SP0", "SP1")
            S("y", "YC", 16, lambda e: e.dma_start(
                dbg["sat"][:, :, :], sat_sp[:, :, :]))

        tag("P4")
        wg_free = [0, 0]
        wu_free = [0, 0]
        wd_free = [0, 0]
        dsl_use = [0, 0, 0]
        wgs_use = [0, 0]
        wus_use = [0, 0]
        wds_use = [0, 0]
        gu_T = 0
        cx_free = 0
        cs_free = 0
        co_free = 0
        W("y", "XQ0", "XQ1", "SP0", "SP1")
        W("g", "V")
        W("t", "SP0", "SP1")

        for tc in range(3):
            tw = TCH[tc]
            t0 = TOFF[tc]
            nmm = (tw + 511) // 512
            WT("y", "T", gu_T if tc > 0 else p3_T)
            S("y", "Q4", 16, lambda e, t0=t0, tw=tw: e.dma_start(
                bass.AP(oaA, XQC_OFF, [[NT * T_TOK, 128], [PSL, 16], [1, tw]]),
                xq_sp[:, :, t0:t0 + tw].rearrange("a p b -> p a b")))

            tag(f"P4.gu{tc}")
            su_rec = {}

            def gu_load_sign(ht, tc=tc):
                sl = ht % 2
                if ht == 0 and tc == 0:
                    W("y", "PP")
                WT("y", "T", wgs_use[sl])
                S("y", ["WG0", "WG1"][sl], 16, lambda e, ht=ht, sl=sl: e.dma_start(
                    r3(wgl[sl][:], 128),
                    wgb[ht, :, :, :].rearrange("a p b -> p a b")))
                WT("y", "T", wus_use[sl])
                S("y", ["WU0", "WU1"][sl], 16, lambda e, ht=ht, sl=sl: e.dma_start(
                    r3(wul[sl][:], 128),
                    wub[ht, :, :, :].rearrange("a p b -> p a b")))
                W("s", ["WG0", "WG1"][sl])
                WT("s", "T", wgs_use[sl])
                S("s", "S", 1, lambda e, sl=sl: e.sign(wgl[sl][:], wgl[sl][:]))
                W("s", ["WU0", "WU1"][sl])
                WT("s", "T", wus_use[sl])
                su_rec[ht] = S("s", "S", 1, lambda e, sl=sl: e.sign(wul[sl][:], wul[sl][:]))

            gu_load_sign(0)
            for ht in range(NHT):
                sl = ht % 2
                psg = ps4a[:, sl * 1024: sl * 1024 + tw]
                psu = ps2a if sl == 0 else ps2b
                gsl = gsil if sl == 0 else gsilB
                WT("t", "S", su_rec[ht])
                if ht == 0:
                    W("t", "Q4")
                need("t", "p4g%d" % sl, "p4d%d" % sl, "ps2a" if sl == 0 else "ps2b")
                for dt in range(NT):
                    for n in range(nmm):
                        w0 = n * 512
                        w1 = min(tw, w0 + 512)
                        S("t", None, 1, lambda e, dt=dt, w0=w0, w1=w1, sl=sl, psg=psg: e.matmul(
                            psg[:, w0:w1],
                            wgl[sl][:, dt * 128:(dt + 1) * 128],
                            bass.AP(oaA, XQC_OFF + dt * PSL + w0,
                                    [[NT * T_TOK, 128], [1, w1 - w0]]),
                            start=(dt == 0), stop=(dt == NT - 1)))
                        S("t", "T" if (dt == NT - 1 and n == nmm - 1) else None, 1,
                          lambda e, dt=dt, w0=w0, w1=w1, sl=sl, psu=psu: e.matmul(
                              psu[:, w0:w1],
                              wul[sl][:, dt * 128:(dt + 1) * 128],
                              bass.AP(oaA, XQC_OFF + dt * PSL + w0,
                                      [[NT * T_TOK, 128], [1, w1 - w0]]),
                              start=(dt == 0), stop=(dt == NT - 1)))
                wgs_use[sl] = C["T"]
                wus_use[sl] = C["T"]
                if ht + 1 < NHT:
                    gu_load_sign(ht + 1)
                W("s", "T")
                need("s", "gsl%d" % sl)
                S("s", "S", 1, lambda e, tw=tw, psg=psg, gsl=gsl: e.activation(
                    gsl[:, 0:tw], psg[:, 0:tw], AF.Silu))
                done("p4g%d" % sl, "S")
                sil = C["S"]
                WT("v", "S", sil)
                S("v", "V", 1, lambda e, ht=ht, tw=tw, psu=psu, gsl=gsl: e.tensor_mul(
                    p_slab(ht)[:, 0:tw], psu[:, 0:tw], gsl[:, 0:tw]))
                done("ps2a" if sl == 0 else "ps2b", "V")
                done("gsl%d" % sl, "V")
            done("pfull", "V")
            gu_T = C["T"]

            tag(f"P4.dn{tc}")
            sd_rec = {}
            NSLAB = NT * 2

            def dn_load_sign(k, tc=tc):
                r = k % 3
                dt_, half = k // 2, k % 2
                if k == 0 and tc == 0:
                    W("g", "PP")
                WT("g", "T", dsl_use[r])
                S("g", ["FD0", "FD1", "FD2"][r], 16,
                  lambda e, dt_=dt_, half=half, r=r: e.dma_start(
                      r3(dsl[r][:], 128),
                      wdb[dt_, half * 32:(half + 1) * 32, :, :].rearrange("a p b -> p a b")))
                W("s", ["FD0", "FD1", "FD2"][r])
                WT("s", "T", dsl_use[r])
                sd_rec[k] = S("s", "S", 1, lambda e, r=r: e.sign(dsl[r][:], dsl[r][:]))

            dsl_use[0] = max(dsl_use[0], gu_T)
            dsl_use[1] = max(dsl_use[1], gu_T)
            dsl_use[2] = max(dsl_use[2], gu_T)
            dn_load_sign(0)
            dn_load_sign(1)
            dn_load_sign(2)
            for G in range(NT * 4):
                dt, g = G // 4, G % 4
                k = G // 2
                r = k % 3
                psd = ps4a[:, (dt % 2) * 1024:(dt % 2) * 1024 + tw]
                if g == 0:
                    if dt < 2:
                        need("t", "pfull", "p4g%d" % (dt % 2))
                    need("t", "p4d%d" % (dt % 2))
                WT("t", "S", sd_rec[k])
                for hidx in range(16):
                    ht = g * 16 + hidx
                    si = (ht % 32)
                    for n in range(nmm):
                        w0 = n * 512
                        w1 = min(tw, w0 + 512)
                        S("t", "T" if (hidx == 15 and n == nmm - 1) else None, 1,
                          lambda e, ht=ht, si=si, r=r, w0=w0, w1=w1, psd=psd: e.matmul(
                              psd[:, w0:w1],
                              dsl[r][:, si * 128:(si + 1) * 128],
                              p_slab(ht)[:, w0:w1],
                              start=(ht == 0), stop=(ht == NHT - 1)))
                if g % 2 == 1:
                    dsl_use[r] = C["T"]
                    if k + 3 < NSLAB:
                        dn_load_sign(k + 3)
                if g == 3:
                    W("s", "T")
                    S("s", "S", 1, lambda e, dt=dt, tw=tw, psd=psd: e.activation(
                        bass.AP(oaA, GF_OFF + dt * PSL, [[NT * T_TOK, 128], [1, tw]]),
                        psd, AF.Silu))
                    done("p4d%d" % (dt % 2), "S")
                    done("gf%d" % dt, "S")
            wgs_use[0] = wgs_use[1] = C["T"]
            wus_use[0] = wus_use[1] = C["T"]

            ntt = tw // 128
            for ttl in range(ntt):
                tt = (t0 // 128) + ttl
                WT("y", "V", cx_free)
                S("y", "CX0", 16, lambda e, tt=tt: e.dma_start(
                    f32a[:], x_d[tt * 128:(tt + 1) * 128, :]))
                WT("y", "V", cs_free)
                S("y", "CS0", 16, lambda e, tt=tt: e.dma_start(
                    oa(SAT_OFF, 2048), sat_sp[tt, :, :]))
                for grp in range(4):
                    pp = ps2a if grp % 2 == 0 else ps2b
                    ppres = "ps2a" if grp % 2 == 0 else "ps2b"
                    need("t", ppres)
                    for j in range(4):
                        dt = grp * 4 + j
                        need("t", "gf%d" % dt)
                        S("t", "T" if j == 3 else None, 1,
                          lambda e, dt=dt, j=j, pp=pp, ttl=ttl: e.matmul(
                              pp[:, j * 128:(j + 1) * 128],
                              bass.AP(oaA, GF_OFF + dt * PSL + ttl * 128,
                                      [[NT * T_TOK, 128], [1, 128]]),
                              cIb[:], start=True, stop=True))
                    W("v", "T")
                    S("v", "V", 1, lambda e, grp=grp, pp=pp: e.tensor_copy(
                        oa(GT_OFF + grp * 512, 512), pp[:, 0:512]))
                    done(ppres, "V")
                W("v", "CX0", "CS0")
                WT("v", "CO0", co_free)
                cs_free = S("v", "V", 1, lambda e: e.tensor_mul(
                    okn[:], oa(GT_OFF, 2048), oa(SAT_OFF, 2048)))
                cx_free = S("v", "V", 1, lambda e: e.tensor_add(
                    f32b[:], okn[:], f32a[:]))
                W("y", "V")
                co_free = S("y", "CO0", 16, lambda e, tt=tt: e.dma_start(
                    out_d[tt * 128:(tt + 1) * 128, :], f32b[:]))

        W("y", "CO0")
        if DEBUG:
            W("y", "YC")

    with es, nc.Block() as block:
        @block.tensor
        def _(t):
            prog("t", t)

        @block.vector
        def _(v):
            prog("v", v)

        @block.scalar
        def _(s):
            prog("s", s)

        @block.gpsimd
        def _(g):
            prog("g", g)

        @block.sync
        def _(y):
            prog("y", y)

    return nc


def host_pack(inputs):
    x = np.ascontiguousarray(np.asarray(inputs["L0_attn_hidden_states"], dtype=np.float32))
    Wq = np.asarray(inputs["L0_attn_Wq"])
    Wk = np.asarray(inputs["L0_attn_Wk"])
    Wv = np.asarray(inputs["L0_attn_Wv"])
    Wg = np.asarray(inputs["L0_attn_Wg"])
    Ws = np.asarray(inputs["L0_attn_Ws"])
    Wo = np.asarray(inputs["L0_attn_Wo"])
    gnw = np.ascontiguousarray(np.asarray(inputs["L0_attn_g_norm_weight"], dtype=np.float32).reshape(128, 1))
    Wgf = np.asarray(inputs["L0_ffn_W_g"])
    Wuf = np.asarray(inputs["L0_ffn_W_u"])
    Wdf = np.asarray(inputs["L0_ffn_W_d"])

    fts = []
    for p in range(8):
        h0, h1 = 2 * p, 2 * p + 1
        fts.append(np.concatenate([Wq[:, h0 * 64:(h0 + 1) * 64], Wq[:, h1 * 64:(h1 + 1) * 64]], 1))
        fts.append(np.concatenate([Wk[:, h0 * 64:(h0 + 1) * 64], Wk[:, h1 * 64:(h1 + 1) * 64]], 1))
        fts.append(np.concatenate([Ws[:, h0 * 64:(h0 + 1) * 64], Ws[:, h1 * 64:(h1 + 1) * 64]], 1))
        fts.append(Wv[:, h0 * 128:(h0 + 1) * 128])
        fts.append(Wv[:, h1 * 128:(h1 + 1) * 128])
        fts.append(Wg[:, h0 * 128:(h0 + 1) * 128])
        fts.append(Wg[:, h1 * 128:(h1 + 1) * 128])
    attn_w = np.ascontiguousarray(np.stack(fts).reshape(56, 16, 128, 128))
    wo_w = np.ascontiguousarray(Wo.reshape(16, 128, D))
    wg_s = np.ascontiguousarray(Wgf.reshape(64, 128, 16, 128).transpose(0, 2, 3, 1))
    wu_s = np.ascontiguousarray(Wuf.reshape(64, 128, 16, 128).transpose(0, 2, 3, 1))
    wd_s = np.ascontiguousarray(Wdf.reshape(16, 128, 64, 128).transpose(0, 2, 3, 1))
    ident_bf = np.eye(128, dtype=ml_dtypes.bfloat16)
    ident_f32 = np.eye(128, dtype=np.float32)
    triu_bf = np.triu(np.ones((128, 128), np.float32)).astype(ml_dtypes.bfloat16)

    shared = dict(attn_w=attn_w, wo_w=wo_w, wg_s=wg_s, wu_s=wu_s, wd_s=wd_s,
                  gnw=gnw, ident_bf=ident_bf, ident_f32=ident_f32,
                  triu_bf=triu_bf)
    return x, shared


def kernel(**inputs):
    x, shared = host_pack(inputs)
    nc = build_nc()
    in_maps = [dict(shared, x=np.ascontiguousarray(x[c])) for c in range(8)]
    res = run_bass_kernel_spmd(nc, in_maps, core_ids=list(range(8)))
    return np.stack([np.asarray(res.results[c]["out"]) for c in range(8)], axis=0)
